# version 14
# baseline (speedup 1.0000x reference)
import numpy as np
import ml_dtypes
import contextlib

import concourse.bass as bass
import concourse.mybir as mybir
from concourse.bass_utils import run_bass_kernel_spmd

F32 = mybir.dt.float32
BF16 = mybir.dt.bfloat16
F8E4 = mybir.dt.float8e4
AF = mybir.ActivationFunctionType
ALU = mybir.AluOpType

LN2 = float(np.log(2.0))
M32 = 12582912.0
M64 = float(1.5 * 2 ** 52)

DEBUG = False

NB = 2
S = 4096
E = 1024
H = 16
D = 64
M = 256
HD = 4096
C = 256
SUB = 128
FC = E // 128
HC = HD // 128
NCHUNK = NB * S // C
CPB = S // C



class Buf:
    __slots__ = ("writer", "readers", "parent")

    def __init__(self, parent=None):
        self.writer = None
        self.readers = []
        self.parent = parent


class Sched:
    STREAMS = ("v", "s", "t", "g", "gd", "sd")
    DMA = {"gd", "sd"}

    def __init__(self):
        self.ops = {k: [] for k in self.STREAMS}
        self.order = []
        self.marked = {k: set() for k in self.STREAMS}
        self.async_ops = set()

    def _ordered(self, prod, cons):
        if prod[0] == cons[0] == "t":
            return True
        return False

    def op(self, stream, emit, reads=(), writes=()):
        reads = list(reads) + [b.parent for b in reads if b.parent is not None]
        writes = list(writes) + [b.parent for b in writes if b.parent is not None]
        idx = len(self.ops[stream])
        me = (stream, idx)
        deps = set()
        for b in reads:
            w = b.writer
            if w is not None and not self._ordered(w, me):
                deps.add(w)
        for b in writes:
            for r in b.readers:
                if not self._ordered(r, me):
                    deps.add(r)
            w = b.writer
            if w is not None and not self._ordered(w, me):
                deps.add(w)
        self.ops[stream].append((emit, sorted(deps)))
        self.order.append(me)
        for b in reads:
            b.readers.append(me)
        for b in writes:
            b.writer = me
            b.readers = []
        for d in deps:
            self.marked[d[0]].add(d[1])
        return me

    def finalize(self):
        self.cum = {}
        for st in self.STREAMS:
            n = len(self.ops[st])
            if st in self.DMA:
                self.marked[st] = set(range(n))
            cum = []
            c = 0
            for i in range(n):
                if i in self.marked[st]:
                    c += 1
                cum.append(c)
            self.cum[st] = cum
        self.batch_end = {}
        for queue in (("g", "gd"), ("sd",)):
            qops = [so for so in self.order if so[0] in queue]
            groups = []
            cur = []
            for so in qops:
                if self.ops[so[0]][so[1]][1] and cur:
                    groups.append(cur)
                    cur = []
                cur.append(so)
            if cur:
                groups.append(cur)
            for grp in groups:
                last = {}
                for st, idx in grp:
                    if st in self.DMA:
                        last[st] = idx
                for st, idx in grp:
                    if st in self.DMA:
                        self.batch_end[(st, idx)] = last[st]

    def wait_value(self, stream, idx):
        if stream in self.DMA:
            return self.cum[stream][self.batch_end[(stream, idx)]] * 16
        return self.cum[stream][idx]


def emit_stream(sched, sems, eng, streams, watermark, force_waits=False):
    mine = [so for so in sched.order if so[0] in streams]
    for st, idx in mine:
        emit, deps = sched.ops[st][idx]
        if st in sched.DMA and deps and idx > 0:
            val = sched.cum[st][idx - 1] * 16
            if watermark.get(st, 0) < val:
                eng.wait_ge(sems[st], val)
                watermark[st] = val
        for dst, didx in deps:
            val = sched.wait_value(dst, didx)
            if force_waits or watermark.get(dst, 0) < val:
                eng.wait_ge(sems[dst], val)
                if watermark.get(dst, 0) < val:
                    watermark[dst] = val
        ins = emit(eng)
        if ins is not None and idx in sched.marked[st]:
            ins.then_inc(sems[st], 16 if st in sched.DMA else 1)




def build_program():
    nc = bass.Bass("TRN2", target_bir_lowering=False)
    dram = {}

    def din(name, shape, dt=F32):
        dram[name] = nc.dram_tensor(name, shape, dt, kind="ExternalInput").ap()
        return dram[name]

    din("x", [NB * E, S])
    din("mask", [NB * E, S], F8E4)
    din("qw", [E, E]); din("kw", [E, E]); din("vw", [E, E]); din("ow", [E, E])
    din("w1t", [E, HD]); din("w2t", [HD, E])
    din("b1r", [128, HC]); din("b2r", [128, FC])
    din("qbr", [128, FC]); din("kbr", [128, FC]); din("obr", [128, FC])
    din("vbrow", [1, E])
    din("gr", [128, FC]); din("br", [128, FC])
    din("projt", [D, M])
    din("identb", [128, 128], BF16)
    din("trilb", [128, 128], BF16)
    din("bones", [128, 128])
    out_h = nc.dram_tensor("out", [NB * E, S], F32, kind="ExternalOutput").ap()
    dbg = {}
    if DEBUG:
        for nm, shape, dt in [("d_t1", [128, FC * C], F32), ("d_t1q", [128, FC * C], F8E4),
                              ("d_h8", [128, HC * C], F8E4), ("d_x1", [128, FC * C], F32),
                              ("d_t2b", [128, FC * C], BF16), ("d_qf", [128, FC * SUB], BF16),
                              ("d_kf", [128, FC * SUB], BF16), ("d_vt", [128, H * (D + 1)], BF16),
                              ("d_crow", [1, H * 128], BF16), ("d_qp", [128, 256], BF16),
                              ("d_kpf", [128, 256], BF16), ("d_kpt", [128, 256], BF16),
                              ("d_at", [128, 128], BF16), ("d_atm", [128, E], BF16),
                              ("d_x2", [128, FC * C], F32), ("d_res", [128, FC * C], F32)]:
            dbg[nm] = nc.dram_tensor(nm, shape, dt, kind="ExternalOutput").ap()

    ex = contextlib.ExitStack()
    with ex:
        B = {}

        def sb(name, shape, dt=F32):
            t = ex.enter_context(nc.sbuf_tensor(name, shape, dt))
            B[name] = Buf()
            return t

        def psum(name, shape, dt=F32):
            t = ex.enter_context(nc.psum_tensor(name, shape, dt))
            B[name] = Buf()
            return t

        wq = sb("wq", [128, FC * E], BF16)
        wk = sb("wk", [128, FC * E], BF16)
        wv = sb("wv", [128, FC * E], BF16)
        wo = sb("wo", [128, FC * E], BF16)
        w1q = sb("w1q", [128, FC * HD], F8E4)
        w2q = sb("w2q", [128, HC * E], F8E4)
        b1q = sb("b1q", [128, HC])
        b2q = sb("b2q", [128, FC])
        qb = sb("qb", [128, FC]); kb = sb("kb", [128, FC]); ob = sb("ob", [128, FC])
        gam = sb("gam", [128, FC]); bet = sb("bet", [128, FC])
        vbt = sb("vbt", [33, 512], BF16)
        projs = sb("projs", [128, M], BF16)
        identb = sb("identb_s", [128, 128], BF16)
        tril = sb("tril_s", [128, 128], BF16)
        bones = sb("bones_s", [128, 128])
        ones1 = sb("ones1", [33, 128])
        ones1b = sb("ones1b", [33, 128], BF16)
        onesc = sb("onesc", [128, 1])
        kons = sb("kons", [128, 13])
        xs = sb("xs", [128, FC * C])
        xs2 = sb("xs2", [128, FC * C])
        mkp = sb("mkp", [128, 1024], F8E4)
        MKR = [Buf(), Buf()]
        t1 = sb("t1", [128, FC * C])
        t2b = sb("t2b", [128, FC * SUB], BF16)
        t1q = sb("t1q", [128, FC * C], F8E4)
        h8 = sb("h8", [128, HC * C], F8E4)
        NSC = 6
        SC = [sb(f"sc{i}", [128, 512]) for i in range(NSC)]
        SCB = [B[f"sc{i}"] for i in range(NSC)]
        T1R = [Buf() for _ in range(4)]
        T1Q = [t1[:, i * 512:(i + 1) * 512] for i in range(4)]

        def t1bufs(c0, c1):
            return [T1R[i] for i in range(4) if i * 512 < c1 and (i + 1) * 512 > c0]
        f1 = sb("f1", [128, FC * C])
        F1R = [Buf() for _ in range(4)]
        F1Q = [f1[:, i * 512:(i + 1) * 512] for i in range(4)]
        aln_off = nc.sbuf_base
        aln = sb("aln", [128, 512])
        ALB = B["aln"]

        def sb_at(name, shape, dt, off):
            t = ex.enter_context(contextlib.nullcontext(
                nc.alloc_sbuf_tensor_at(name, shape, dt, offset=off)))
            B[name] = Buf(parent=ALB)
            return t.ap()
        qp = sb_at("qp", [128, 256], BF16, aln_off)
        kpf = sb_at("kpf", [128, 256], BF16, aln_off + 512)
        at = sb_at("at", [128, 128], BF16, aln_off + 1024)
        qf = sb("qf", [128, FC * SUB], BF16)
        kf = sb("kf", [128, FC * SUB], BF16)
        vt = sb("vt", [128, H * (D + 1)], BF16)
        kvs = sb("kvs", [128, H * 2 * (D + 1)])
        kvbh = sb("kvbh", [128, 2 * (D + 1)], BF16)
        corr = sb("corr", [16, 128], BF16)
        corr_row = sb("corr_row", [33, 8 * 128], BF16)
        corrt = sb("corrt", [128, 16])
        rcp = sb("rcp", [128, 1])

        pb0 = psum("pb0", [128, 256]); pb1 = psum("pb1", [128, 256])
        pb2 = psum("pb2", [128, 256]); pb3 = psum("pb3", [128, 256])
        pb4 = psum("pb4", [128, 512]); pb5 = psum("pb5", [128, 512])
        pb6 = psum("pb6", [128, 512]); pb7 = psum("pb7", [128, 1024], BF16)
        def view(name, parent, ap):
            B[name] = Buf(parent=B[parent])
            return ap
        pmm = [view("pmm0", "pb0", pb0[:, 0:256]), view("pmm1", "pb1", pb1[:, 0:256])]
        pm2 = [view("pm20", "pb2", pb2[:, 0:256]), view("pm21", "pb3", pb3[:, 0:256])]
        pkv = view("pkv", "pb5", pb5[:, 0:2 * (D + 1)])
        pkv1 = view("pkv1", "pb3", pb3[:, 0:2 * (D + 1)])
        pO1 = view("pO1", "pb2", pb2[:, 0:128])
        pbc = view("pbc", "pb4", pb4[:, 0:512])
        pkp = view("pkp", "pb2", pb2[:, 0:256])
        prow = view("prow", "pb4", pb4[0:1, 0:512])
        pcorr = view("pcorr", "pb5", pb5[0:16, 0:128])
        psc = view("psc", "pb5", pb5[:, 144:272])
        pout = view("pout", "pb5", pb5[:, 272:272 + D + 1])
        pO = view("pO", "pb5", pb5[:, 384:512])
        pqp = view("pqp", "pb6", pb6[:, 0:256])
        pkt = view("pkt", "pb6", pb6[:, 256:512])
        prow_a = view("prow_a", "pb6", pb6[0:1, 0:512])
        pbc_a = view("pbc_a", "pb6", pb6[:, 0:512])
        pm2row = view("pm2row", "pb3", pb3[0:1, 0:256])
        ptr = view("ptr", "pb7", pb7[:, 0:128])
        pct = view("pct", "pb7", pb7[:, 128:144])

        SETA = [(SC[i], SCB[i]) for i in range(4)]
        SETB = [(SC[4], SCB[4]), (SC[5], SCB[5]),
                (F1Q[2], F1R[2]), (F1Q[3], F1R[3])]
        VA = (F1Q[0], F1R[0])
        VB = (F1Q[1], F1R[1])

        sems = {k: ex.enter_context(nc.semaphore(f"sem_{k}")) for k in Sched.STREAMS}
        block = ex.enter_context(nc.Block())
        S_ = Sched()
        SINK = [None]

        def emit_op(stream, f, reads=(), writes=()):
            if SINK[0] is None:
                return S_.op(stream, f, reads=reads, writes=writes)
            SINK[0].append((stream, f, list(reads), list(writes)))
            return None

        import itertools

        def run_interleaved(*queues):
            queues = [q for q in queues if q]
            if not queues:
                return
            lens = [len(q) for q in queues]
            idx = [0] * len(queues)
            for _ in range(sum(lens)):
                i = min((j for j in range(len(queues)) if idx[j] < lens[j]),
                        key=lambda j: idx[j] / lens[j])
                item = queues[i][idx[i]]
                idx[i] += 1
                if SINK[0] is not None:
                    SINK[0].append(item)
                else:
                    S_.op(item[0], item[1], reads=item[2], writes=item[3])

        def _bl(x):
            if x is None:
                return []
            return list(x) if isinstance(x, (list, tuple)) else [x]

        def ts(stream, out, outb, in_, inb, s1, op0, s2=None, op1=None, extra=()):
            def f(e, out=out, in_=in_, s1=s1, s2=s2, op0=op0, op1=op1):
                if op1 is None:
                    return e.tensor_scalar(out=out, in0=in_, scalar1=s1, scalar2=None,
                                           op0=op0)
                return e.tensor_scalar(out=out, in0=in_, scalar1=s1, scalar2=s2,
                                       op0=op0, op1=op1)
            return emit_op(stream, f, reads=_bl(inb) + list(extra), writes=_bl(outb))

        def tt(stream, out, outb, a, ab, b, bb, op):
            def f(e, out=out, a=a, b=b, op=op):
                return e.tensor_tensor(out=out, in0=a, in1=b, op=op)
            return emit_op(stream, f, reads=_bl(ab) + _bl(bb), writes=_bl(outb))

        def stt(stream, out, outb, a, ab, scal, scalb, b, bb, op0, op1):
            def f(e, out=out, a=a, scal=scal, b=b, op0=op0, op1=op1):
                return e.scalar_tensor_tensor(out=out, in0=a, scalar=scal, in1=b,
                                              op0=op0, op1=op1)
            return emit_op(stream, f, reads=_bl(ab) + _bl(bb) + _bl(scalb),
                           writes=_bl(outb))

        def act(out, outb, in_, inb, func, bias=None, biasb=None, scale=None,
                scaleb=None):
            kw = {}
            rd = _bl(inb)
            if bias is not None:
                kw["bias"] = bias
                rd += _bl(biasb)
            if scale is not None:
                kw["scale"] = scale
                rd += _bl(scaleb)

            def f(e, out=out, in_=in_, func=func, kw=kw):
                return e.activation(out=out, in_=in_, func=func, **kw)
            return emit_op("s", f, reads=rd, writes=_bl(outb))

        def mm(out, outb, lhsT, lb, rhs, rb, start, stop):
            def f(e, out=out, lhsT=lhsT, rhs=rhs, start=start, stop=stop):
                return e.matmul(out, lhsT, rhs, start=start, stop=stop)
            return emit_op("t", f, reads=_bl(lb) + _bl(rb), writes=_bl(outb))

        def transpose(out, outb, in_, inb, ident, identbuf):
            def f(e, out=out, in_=in_, ident=ident):
                return e.transpose(out, in_, ident)
            return emit_op("t", f, reads=_bl(inb) + _bl(identbuf), writes=_bl(outb))

        def dma(stream, out, outb, in_, inb):
            def f(e, out=out, in_=in_):
                return e.dma_start(out=out, in_=in_)
            return emit_op(stream, f, reads=_bl(inb), writes=_bl(outb))

        def copy(stream, out, outb, in_, inb):
            if stream == "s":
                def f(e, out=out, in_=in_):
                    return e.activation(out=out, in_=in_, func=AF.Copy)
            else:
                def f(e, out=out, in_=in_):
                    return e.tensor_copy(out=out, in_=in_)
            return emit_op(stream, f, reads=_bl(inb), writes=_bl(outb))

        def memset(stream, ap, apb, val):
            def f(e, ap=ap, val=val):
                return e.memset(ap, val)
            op = S_.op(stream, f, reads=[], writes=_bl(apb))
            S_.async_ops.add(op)
            return op

        KB = B["kons"]
        GF = kons[:, 2:3]; LNT = kons[:, 3:4]; LNNEG = kons[:, 4:5]
        B170P = kons[:, 5:6]; B170N = kons[:, 6:7]
        B163P = kons[:, 7:8]; B163N = kons[:, 8:9]; B8P = kons[:, 9:10]
        EPS5 = kons[0:1, 10:11]; EPS6 = kons[0:1, 11:12]; GI = kons[:, 12:13]

        def bridge(dst, dstb, src, srcb, sA, sAb, sB, sBb):
            ts("v", sA, sAb, src, srcb, M32, ALU.add, M32, ALU.subtract)
            ts("v", sB, sBb, src, srcb, M64, ALU.add, M64, ALU.subtract)
            stt("v", dst, dstb, sA, sAb, GF, KB, sB, sBb, ALU.mult, ALU.add)

        def qsig(dst, dstb, src, srcb, W, SET, mb):
            (s1, b1_), (s2, b2_), (s3, b3_), (s4, b4_) = [
                (a[:, :W], bu) for a, bu in SET]
            bp = B163P if mb == 3 else B170P
            bn = B163N if mb == 3 else B170N
            clamp = 2.0 ** -6 if mb == 3 else 2.0 ** -14
            stt("v", s1, b1_, src, srcb, -1.0, None, src, srcb, ALU.mult, ALU.max)
            ts("v", s1, b1_, s1, b1_, clamp, ALU.max)
            act(s2, b2_, s1, b1_, AF.Ln)
            ts("v", s2, b2_, s2, b2_, 1.0 / LN2, ALU.mult, 159.5, ALU.add)
            ts("v", s3, b3_, s2, b2_, M32, ALU.add, M32, ALU.subtract)
            ts("v", s4, b4_, s2, b2_, M64, ALU.add, M64, ALU.subtract)
            stt("v", s1, b1_, s3, b3_, GF, KB, s4, b4_, ALU.mult, ALU.add)
            act(s3, b3_, s1, b1_, AF.Exp, scale=LNNEG, scaleb=KB, bias=bp, biasb=KB)
            act(s4, b4_, s1, b1_, AF.Exp, scale=LNT, scaleb=KB, bias=bn, biasb=KB)
            tt("v", s2, b2_, src, srcb, s3, b3_, ALU.mult)
            ts("v", s3, b3_, s2, b2_, M32, ALU.add, M32, ALU.subtract)
            ts("v", s1, b1_, s2, b2_, M64, ALU.add, M64, ALU.subtract)
            stt("v", s2, b2_, s3, b3_, GF, KB, s1, b1_, ALU.mult, ALU.add)
            tt("v", dst, dstb, s2, b2_, s4, b4_, ALU.mult)

        def q43_signed(dst, dstb, src, srcb, W, SET=None):
            qsig(dst, dstb, src, srcb, W, SET or SETA, 3)

        def q510_signed(dst, dstb, src, srcb, W, SET=None):
            qsig(dst, dstb, src, srcb, W, SET or SETA, 10)

        def h_chain(dst, dstb, v_, vb, W, SET):
            (s1, b1_), (s2, b2_), (s3, b3_), (s4, b4_) = [
                (a[:, :W], bu) for a, bu in SET]
            ts("v", s1, b1_, v_, vb, 2.0 ** -14, ALU.max)
            act(s2, b2_, s1, b1_, AF.Ln)
            ts("v", s2, b2_, s2, b2_, 1.0 / LN2, ALU.mult, 159.5, ALU.add)
            ts("v", s3, b3_, s2, b2_, M32, ALU.add, M32, ALU.subtract)
            ts("v", s4, b4_, s2, b2_, M64, ALU.add, M64, ALU.subtract)
            stt("v", s1, b1_, s3, b3_, GF, KB, s4, b4_, ALU.mult, ALU.add)
            act(s3, b3_, s1, b1_, AF.Exp, scale=LNNEG, scaleb=KB, bias=B170P, biasb=KB)
            tt("v", v_, vb, v_, vb, s3, b3_, ALU.mult)
            ts("v", s2, b2_, v_, vb, M32, ALU.add, M32, ALU.subtract)
            ts("v", s4, b4_, v_, vb, M64, ALU.add, M64, ALU.subtract)
            stt("v", s3, b3_, s2, b2_, GF, KB, s4, b4_, ALU.mult, ALU.add)
            ts("v", s2, b2_, s1, b1_, -154.0, ALU.add, 0.0, ALU.min)
            ts("v", s4, b4_, s1, b1_, 154.0, ALU.max)
            act(s2, b2_, s2, b2_, AF.Exp, scale=LNT, scaleb=KB, bias=B8P, biasb=KB)
            ts("v", s1, b1_, s2, b2_, M32, ALU.add, M32, ALU.subtract)
            ts("v", v_, vb, s2, b2_, M64, ALU.add, M64, ALU.subtract)
            stt("v", s2, b2_, s1, b1_, GF, KB, v_, vb, ALU.mult, ALU.add)
            act(s4, b4_, s4, b4_, AF.Exp, scale=LNT, scaleb=KB, bias=B163N, biasb=KB)
            tt("v", s1, b1_, s3, b3_, s2, b2_, ALU.mult)
            ts("v", s1, b1_, s1, b1_, 2.0 ** -15, ALU.mult)
            ts("v", s3, b3_, s1, b1_, M32, ALU.add, M32, ALU.subtract)
            ts("v", s2, b2_, s1, b1_, M64, ALU.add, M64, ALU.subtract)
            stt("v", s1, b1_, s3, b3_, GF, KB, s2, b2_, ALU.mult, ALU.add)
            tt("v", dst, dstb, s1, b1_, s4, b4_, ALU.mult)

        def ln_stats(x_t, xb, eps_ap, SCR=None, PB=None):
            if SCR is None:
                SCR = ((SC[0][:, 0:C], SCB[0]), (SC[1][:, 0:C], SCB[1]),
                       (SC[2][0:1, 0:512], SCB[2]))
            (sq0, sq0b), (sq1, sq1b), (r2t, r2b) = SCR[0], SCR[1], (SCR[2][0], SCR[2][1])
            mu_, mub, m2_, m2b, pbc_, pbcb = PB or (
                prow[0:1, 0:C], B["prow"], prow[0:1, C:2 * C], B["prow"],
                pbc, B["pbc"])
            def xq(k):
                return xb[(k * C) // 512] if isinstance(xb, list) else xb
            for k in range(FC):
                mm(mu_, mub, onesc[:, 0:1], B["onesc"],
                   x_t[:, k * C:(k + 1) * C], xq(k), start=(k == 0), stop=(k == FC - 1))
            for k in range(FC):
                sq, sqb = (sq0, sq0b) if k % 2 == 0 else (sq1, sq1b)
                act(sq, sqb, x_t[:, k * C:(k + 1) * C], xq(k), AF.Square)
                mm(m2_, m2b, onesc[:, 0:1], B["onesc"],
                   sq, sqb, start=(k == 0), stop=(k == FC - 1))
            ln_stats_tail(eps_ap, SCR, (mu_, mub, m2_, m2b, pbc_, pbcb))

        def ln_stats_rows(x_t, xb, k, SCR, PB):
            (sq0, sq0b), (sq1, sq1b), _ = SCR
            mu_, mub, m2_, m2b, pbc_, pbcb = PB
            mm(mu_, mub, onesc[:, 0:1], B["onesc"],
               x_t[:, k * C:(k + 1) * C], xb, start=(k == 0), stop=(k == FC - 1))
            sq, sqb = (sq0, sq0b) if k % 2 == 0 else (sq1, sq1b)
            act(sq, sqb, x_t[:, k * C:(k + 1) * C], xb, AF.Square)
            mm(m2_, m2b, onesc[:, 0:1], B["onesc"],
               sq, sqb, start=(k == 0), stop=(k == FC - 1))

        def ln_stats_tail(eps_ap, SCR, PB):
            (sq0, sq0b), (sq1, sq1b), (r2t, r2b) = SCR[0], SCR[1], (SCR[2][0], SCR[2][1])
            mu_, mub, m2_, m2b, pbc_, pbcb = PB
            def a_mu(e):
                return e.activation(out=r2t[:, 0:C], in_=mu_,
                                    func=AF.Copy, scale=1.0 / 1024)
            emit_op("s", a_mu, reads=[mub], writes=[r2b])
            def a_m2(e):
                return e.activation(out=r2t[:, C:2 * C], in_=m2_,
                                    func=AF.Copy, scale=1.0 / 1024)
            emit_op("s", a_m2, reads=[m2b], writes=[r2b])
            mm(pbc_[:, 0:C], pbcb, ones1[0:1, :], B["ones1"], r2t[:, 0:C], r2b,
               start=True, stop=True)
            tt("v", r2t[:, 0:C], r2b, r2t[:, 0:C], r2b,
               r2t[:, 0:C], r2b, ALU.mult)
            tt("v", r2t[:, 0:C], r2b, r2t[:, C:2 * C], r2b,
               r2t[:, 0:C], r2b, ALU.subtract)
            act(r2t[:, 0:C], r2b, r2t[:, 0:C], r2b,
                AF.Ln, bias=eps_ap, biasb=KB)
            def a_rstd(e):
                return e.activation(out=r2t[:, 0:C], in_=r2t[:, 0:C],
                                    func=AF.Exp, scale=-0.5)
            emit_op("s", a_rstd, reads=[r2b], writes=[r2b])
            mm(pbc_[:, C:2 * C], pbcb, ones1[0:1, :], B["ones1"], r2t[:, 0:C],
               r2b, start=True, stop=True)

        def ln_normalize(dst_t, dstb, x_t, xb, PB=None):
            pbc_, pbcb = (PB[4], PB[5]) if PB else (pbc, B["pbc"])
            def bq(bb, k):
                return bb[(k * C) // 512] if isinstance(bb, list) else bb
            for k in range(FC):
                sl = slice(k * C, (k + 1) * C)
                tt("v", dst_t[:, sl], bq(dstb, k), x_t[:, sl], bq(xb, k),
                   pbc_[:, 0:C], pbcb, ALU.subtract)
                tt("v", dst_t[:, sl], bq(dstb, k), dst_t[:, sl], bq(dstb, k),
                   pbc_[:, C:2 * C], pbcb, ALU.mult)

        memset("v", kons[:, 0:1], KB, 2.0)
        memset("v", kons[:, 1:2], KB, 1.0)
        ts("v", kons[:, 2:3], KB, kons[:, 1:2], KB, 2.0 ** -40, ALU.add, 1.0, ALU.is_equal)
        ts("v", kons[:, 12:13], KB, kons[:, 2:3], KB, -1.0, ALU.mult, 1.0, ALU.add)
        act(kons[:, 3:4], KB, kons[:, 0:1], KB, AF.Ln)
        for col, mul in [(4, -1.0), (5, 170.0), (6, -170.0), (7, 163.0), (8, -163.0),
                         (9, 8.0)]:
            ts("v", kons[:, col:col + 1], KB, kons[:, 3:4], KB, mul, ALU.mult)
        memset("v", kons[:, 10:11], KB, 1e-5)
        memset("v", kons[:, 11:12], KB, 1e-6)
        memset("v", ones1[0:1, :], B["ones1"], 1.0)
        memset("v", ones1[32:33, :], B["ones1"], 1.0)
        memset("v", ones1b[0:1, :], B["ones1b"], 1.0)
        memset("v", ones1b[32:33, :], B["ones1b"], 1.0)
        memset("v", onesc[:], B["onesc"], 1.0)

        for name, tile, tb in [("identb", identb, B["identb_s"]),
                               ("trilb", tril, B["tril_s"]),
                               ("bones", bones, B["bones_s"]),
                               ("qbr", qb, B["qb"]), ("kbr", kb, B["kb"]),
                               ("obr", ob, B["ob"]), ("gr", gam, B["gam"]),
                               ("br", bet, B["bet"])]:
            dma("sd", tile[:], tb, dram[name], Buf())

        dma("sd", SC[3][0:1, 0:512], SCB[3], dram["vbrow"][0:1, 0:512], Buf())
        dma("sd", SC[4][0:1, 0:512], SCB[4], dram["vbrow"][0:1, 512:1024], Buf())
        copy("s", vbt[0:1, :], B["vbt"], SC[3][0:1, 0:512], SCB[3])
        copy("s", vbt[32:33, :], B["vbt"], SC[4][0:1, 0:512], SCB[4])

        dma("sd", SC[5][0:64, 0:M], SCB[5], dram["projt"], Buf())
        dma("sd", SC[5][64:128, 0:M], SCB[5], dram["projt"], Buf())
        ts("v", projs[:, :], B["projs"], SC[5][:, 0:M], SCB[5], 0.125, ALU.mult)

        dma("sd", SC[5][:, 256:256 + HC], SCB[5], dram["b1r"], Buf())
        q43_signed(b1q[:, :], B["b1q"], SC[5][:, 256:256 + HC], SCB[5], HC)
        dma("sd", SC[5][:, 256:256 + FC], SCB[5], dram["b2r"], Buf())
        q43_signed(b2q[:, :], B["b2q"], SC[5][:, 256:256 + FC], SCB[5], FC)

        for wname, wtile, wb in [("qw", wq, B["wq"]), ("kw", wk, B["wk"]),
                                 ("vw", wv, B["wv"]), ("ow", wo, B["wo"])]:
            for i in range(0, FC, 2):
                src = dram[wname][i * 128:(i + 2) * 128, :].rearrange(
                    "(k p) n -> p k n", p=128)
                dma("sd", xs[:, 0:2048].rearrange("p (k n) -> p k n", k=2), B["xs"],
                    src, Buf())
                copy("v", wtile[:, i * E:(i + 2) * E], wb, xs[:, 0:2048], B["xs"])

        def wquant4(dst_tile, dstb, col0):
            outer = SINK[0]
            qa, qb2 = [], []
            for q in range(4):
                SINK[0] = qa if q % 2 == 0 else qb2
                q43_signed(dst_tile[:, col0 + q * 512:col0 + (q + 1) * 512], dstb,
                           xs[:, q * 512:(q + 1) * 512], B["xs"], 512,
                           SET=(SETA if q % 2 == 0 else SETB))
            SINK[0] = outer
            run_interleaved(qa, qb2)

        for i in range(FC):
            for half in range(2):
                dma("sd", xs[:, 0:2048], B["xs"],
                    dram["w1t"][i * 128:(i + 1) * 128, half * 2048:(half + 1) * 2048],
                    Buf())
                wquant4(w1q, B["w1q"], i * HD + half * 2048)
        for j in range(0, HC, 2):
            src = dram["w2t"][j * 128:(j + 2) * 128, :].rearrange(
                "(k p) n -> p k n", p=128)
            dma("sd", xs[:, 0:2048].rearrange("p (k n) -> p k n", k=2), B["xs"],
                src, Buf())
            wquant4(w2q, B["w2q"], j * E)

        for h in range(H):
            memset("v", vt[:, h * (D + 1) + D:h * (D + 1) + D + 1], B["vt"], 1.0)

        def dump(nm, tile, tileb):
            if DEBUG and nm in dbg and nm not in dump.done:
                dump.done.add(nm)
                dma("gd", dbg[nm], Buf(), tile, tileb)
        dump.done = set()

        def ffn_sublayer(b, trow, xs_t, xsb, tag=""):
            ln_stats(xs_t[:, :], xsb, EPS5)
            ln_normalize(f1[:, :], F1R, xs_t[:, :], xsb)
            if tag == "f1":
                dump("d_t1", f1[:, :], F1R)
            for q in (2, 3):
                q43_signed(t1q[:, q * 512:(q + 1) * 512], B["t1q"],
                           F1Q[q], F1R[q], 512, SET=SETA)
            outer = SINK[0]
            qa, qb2 = [], []
            SINK[0] = qa
            q43_signed(t1q[:, 0:512], B["t1q"], F1Q[0], F1R[0], 512, SET=SETA)
            SINK[0] = qb2
            q43_signed(t1q[:, 512:1024], B["t1q"], F1Q[1], F1R[1], 512, SET=SETB)
            SINK[0] = outer
            run_interleaved(qa, qb2)
            if tag == "f1":
                dump("d_t1q", t1q[:, :], B["t1q"])
            for j in range(0, HC, 4):
                for pair, (vt_, vb_) in ((0, VA), (2, VB)):
                    for jj in (j + pair, j + pair + 1):
                        p = pmm[jj % 2]
                        pb = B[f"pmm{jj % 2}"]
                        for i in range(FC):
                            mm(p[:, 0:C], pb,
                               w1q[:, i * HD + jj * 128:i * HD + jj * 128 + 128],
                               B["w1q"], t1q[:, i * C:(i + 1) * C], B["t1q"],
                               start=(i == 0), stop=(i == FC - 1))
                        half = jj - j - pair
                        def ev(e, p=p, jj=jj, half=half, vt_=vt_):
                            return e.activation(out=vt_[:, half * C:(half + 1) * C],
                                                in_=p[:, 0:C], func=AF.Relu,
                                                bias=b1q[:, jj:jj + 1])
                        emit_op("s", ev, reads=[pb, B["b1q"]], writes=[vb_])
                outer = SINK[0]
                qa, qb2 = [], []
                SINK[0] = qa
                h_chain(h8[:, j * C:(j + 2) * C], B["h8"], VA[0], VA[1], 2 * C, SETA)
                SINK[0] = qb2
                h_chain(h8[:, (j + 2) * C:(j + 4) * C], B["h8"], VB[0], VB[1], 2 * C,
                        SETB)
                SINK[0] = outer
                run_interleaved(qa, qb2)
            if tag == "f1":
                dump("d_h8", h8[:, :], B["h8"])
            for d in (0, 4):
                for pair, (vt_, vb_), mh in ((0, VA, 0), (2, VB, 1)):
                    for dd in (d + pair, d + pair + 1):
                        p = pmm[dd % 2]
                        pb = B[f"pmm{dd % 2}"]
                        for jj in range(HC):
                            mm(p[:, 0:C], pb,
                               w2q[:, jj * E + dd * 128:jj * E + dd * 128 + 128],
                               B["w2q"], h8[:, jj * C:(jj + 1) * C], B["h8"],
                               start=(jj == 0), stop=(jj == HC - 1))
                        half = dd - d - pair
                        def ev2(e, p=p, dd=dd, half=half, vt_=vt_):
                            return e.activation(out=vt_[:, half * C:(half + 1) * C],
                                                in_=p[:, 0:C], func=AF.Identity,
                                                bias=b2q[:, dd:dd + 1])
                        emit_op("s", ev2, reads=[pb, B["b2q"]], writes=[vb_])
                    msrc = dram["mask"][b * E + (d + pair) * 128:
                                        b * E + (d + pair + 2) * 128,
                                        trow:trow + C].rearrange(
                        "(k p) t -> p k t", p=128)
                    dma("sd", mkp[:, mh * 512:(mh + 1) * 512].rearrange(
                        "p (k t) -> p k t", k=2), MKR[mh], msrc, Buf())
                outer = SINK[0]
                qa, qb2 = [], []
                for pair, (vt_, vb_), mh, st_, q_ in ((0, VA, 0, SETA, 0),
                                                      (2, VB, 1, SETB, 1)):
                    SINK[0] = qa if q_ == 0 else qb2
                    q510_signed(vt_[:, 0:2 * C], vb_, vt_[:, 0:2 * C], vb_, 2 * C,
                                SET=st_)
                    tt("g", vt_[:, 0:2 * C], vb_, vt_[:, 0:2 * C], vb_,
                       mkp[:, mh * 512:(mh + 1) * 512], MKR[mh], ALU.mult)
                    sl = slice((d + pair) * C, (d + pair + 2) * C)
                    tt("g", xs_t[:, sl], xsb, xs_t[:, sl], xsb,
                       vt_[:, 0:2 * C], vb_, ALU.add)
                SINK[0] = outer
                run_interleaved(qa, qb2)

        ASCR = ((aln[:, 0:C], ALB), (aln[:, C:2 * C], ALB),
                (aln[0:1, 0:512], ALB))

        def attn_sublayer(cix, xs_t, xsb):
            batch_start = (cix % CPB) == 0
            if cix == 0:
                dump("d_x1", xs_t[:, :], xsb)
            APB = (prow_a[0:1, 0:C], B["prow_a"], pm2row[0:1, 0:C], B["pm2row"],
                   pbc_a, B["pbc_a"])
            ln_stats(xs_t[:, :], xsb, EPS5, SCR=ASCR, PB=APB)
            ln_normalize(t1[:, :], T1R, xs_t[:, :], xsb, PB=APB)
            for a in range(C // SUB):
                first = batch_start and a == 0
                copy("s", t2b[:, :].rearrange("p (k t) -> p k t", t=SUB), B["t2b"],
                     t1[:, :].rearrange("p (k t) -> p k t", t=C)[:, :, a * SUB:(a + 1) * SUB],
                     T1R)
                if cix == 0 and a == 0:
                    dump("d_t2b", t2b[:, :], B["t2b"])
                for wtile, wb, btile, bb, dstt, dstb in (
                        (wq, B["wq"], qb, B["qb"], qf, B["qf"]),
                        (wk, B["wk"], kb, B["kb"], kf, B["kf"])):
                    for dk in range(FC):
                        p = pm2[dk % 2]
                        pb = B[f"pm2{dk % 2}"]
                        for i in range(FC):
                            mm(p[:, 0:SUB], pb,
                               wtile[:, i * E + dk * 128:i * E + dk * 128 + 128], wb,
                               t2b[:, i * SUB:(i + 1) * SUB], B["t2b"],
                               start=(i == 0), stop=(i == FC - 1))
                        def evq(e, p=p, dk=dk, dstt=dstt, btile=btile):
                            return e.activation(out=dstt[:, dk * SUB:(dk + 1) * SUB],
                                                in_=p[:, 0:SUB], func=AF.Identity,
                                                bias=btile[:, dk:dk + 1])
                        emit_op("s", evq, reads=[pb, bb], writes=[dstb])
                for half in range(2):
                    vbase = 32 * half
                    for q2 in range(2):
                        p = pm2[q2]
                        pbq = B[f"pm2{q2}"]
                        off = half * 512 + q2 * 256
                        for i in range(FC):
                            mm(p[:, 0:256], pbq,
                               t2b[:, i * SUB:(i + 1) * SUB], B["t2b"],
                               wv[:, i * E + off:i * E + off + 256], B["wv"],
                               start=(i == 0), stop=False)
                        mm(p[:, 0:256], pbq,
                           ones1b[vbase:vbase + 1, :], B["ones1b"],
                           vbt[vbase:vbase + 1, q2 * 256:(q2 + 1) * 256], B["vbt"],
                           start=False, stop=True)
                        hb = half * 8 + q2 * 4
                        dstv = vt[:, hb * (D + 1):(hb + 4) * (D + 1)].rearrange(
                            "p (h d) -> p h d", d=D + 1)[:, :, 0:D]
                        def evv(e, p=p, dstv=dstv):
                            return e.activation(
                                out=dstv,
                                in_=p[:, 0:256].rearrange("p (h d) -> p h d", d=D),
                                func=AF.Copy)
                        emit_op("s", evv, reads=[pbq], writes=[B["vt"]])
                if cix == 0 and a == 0:
                    dump("d_qf", qf[:, :], B["qf"])
                    dump("d_kf", kf[:, :], B["kf"])
                    dump("d_vt", vt[:, :], B["vt"])
                for fc2 in range(FC):
                    sqk = aln[:, (fc2 % 2) * 128:(fc2 % 2) * 128 + 128]
                    act(sqk, ALB, kf[:, fc2 * 128:(fc2 + 1) * 128], B["kf"],
                        AF.Square)
                    mm(pcorr[:, :], B["pcorr"], bones[:, fc2 * 16:(fc2 + 1) * 16],
                       B["bones_s"], sqk, ALB,
                       start=(fc2 == 0), stop=(fc2 == FC - 1))
                def evc(e):
                    return e.activation(out=corr[:, :], in_=pcorr[:, :], func=AF.Copy,
                                        scale=-1.0 / 128)
                emit_op("s", evc, reads=[B["pcorr"]], writes=[B["corr"]])
                dma("gd", corr_row[0:1, :], B["corr_row"], corr[0:8, :], B["corr"])
                dma("gd", corr_row[32:33, :], B["corr_row"], corr[8:16, :], B["corr"])
                transpose(pct[:, :], B["pct"], corr[:, :], B["corr"],
                          identb[0:16, 0:16], B["identb_s"])
                copy("s", corrt[:, :], B["corrt"], pct[:, :], B["pct"])
                for h in range(H):
                    fc2 = h // 2
                    base = 64 * (h % 2)
                    Qh = qf[base:base + 64, fc2 * SUB:(fc2 + 1) * SUB]
                    Kh = kf[base:base + 64, fc2 * SUB:(fc2 + 1) * SUB]
                    ksl = slice(h * 2 * (D + 1), (h + 1) * 2 * (D + 1))
                    if not first:
                        copy("s", kvbh[:, :], B["kvbh"], kvs[:, ksl], B["kvs"])
                    for mc in range(2):
                        mm(pqp[:, mc * 128:(mc + 1) * 128], B["pqp"],
                           projs[base:base + 64, mc * 128:(mc + 1) * 128], B["projs"],
                           Qh, B["qf"], start=True, stop=True)
                    act(qp[:, 0:256], B["qp"], pqp[:, 0:256], B["pqp"], AF.Exp)
                    for mc in range(2):
                        mm(pkp[:, mc * 128:(mc + 1) * 128], B["pkp"],
                           projs[base:base + 64, mc * 128:(mc + 1) * 128], B["projs"],
                           Kh, B["kf"], start=True, stop=False)
                        cbase = 32 * (h // 8)
                        mm(pkp[:, mc * 128:(mc + 1) * 128], B["pkp"],
                           ones1b[cbase:cbase + 1, :], B["ones1b"],
                           corr_row[cbase:cbase + 1, (h % 8) * 128:(h % 8) * 128 + 128],
                           B["corr_row"], start=False, stop=True)
                    act(kpf[:, 0:256], B["kpf"], pkp[:, 0:256], B["pkp"], AF.Exp)
                    mm(pkt[:, 0:256], B["pkt"], Kh, B["kf"],
                       projs[base:base + 64, 0:256], B["projs"], start=True, stop=True)
                    for mc in range(2):
                        mm(psc[:, :], B["psc"], kpf[:, mc * 128:(mc + 1) * 128], B["kpf"],
                           qp[:, mc * 128:(mc + 1) * 128], B["qp"],
                           start=(mc == 0), stop=(mc == 1))
                    tt("v", at[:, :], B["at"], psc[:, :], B["psc"], tril[:],
                       B["tril_s"], ALU.mult)
                    act(kpf[:, 0:256], B["kpf"], pkt[:, 0:256], B["pkt"], AF.Exp,
                        bias=corrt[:, h:h + 1], biasb=B["corrt"])
                    if cix == 0 and a == 0 and h == 0:
                        dump("d_qp", qp[:, :], B["qp"])
                        dump("d_at", at[:, :], B["at"])
                    Vh = vt[:, h * (D + 1):(h + 1) * (D + 1)]
                    mm(pout[:, :], B["pout"], at[:, :], B["at"], Vh, B["vt"],
                       start=True, stop=first)
                    if not first:
                        for mc in range(2):
                            mm(pout[:, :], B["pout"], qp[:, mc * 128:(mc + 1) * 128],
                               B["qp"],
                               kvbh[:, mc * (D + 1):(mc + 1) * (D + 1)],
                               B["kvbh"], start=False, stop=(mc == 1))
                    pkv_, pkvb = (pkv, B["pkv"]) if h % 2 == 0 else (pkv1, B["pkv1"])
                    for mc in range(2):
                        mm(pkv_[:, mc * (D + 1):(mc + 1) * (D + 1)], pkvb,
                           kpf[:, mc * 128:(mc + 1) * 128], B["kpf"], Vh, B["vt"],
                           start=True, stop=True)
                    if first:
                        copy("v", kvs[:, ksl], B["kvs"], pkv_[:, :], pkvb)
                    else:
                        tt("v", kvs[:, ksl], B["kvs"], kvs[:, ksl], B["kvs"],
                           pkv_[:, :], pkvb, ALU.add)
                    def evr(e):
                        return e.reciprocal(out=rcp[:, :], in_=pout[:, D:D + 1])
                    emit_op("v", evr, reads=[B["pout"]], writes=[B["rcp"]])
                    def evo(e, h=h):
                        return e.activation(out=t2b[:, h * D:(h + 1) * D],
                                            in_=pout[:, 0:D], func=AF.Copy,
                                            scale=rcp[:, 0:1])
                    emit_op("s", evo, reads=[B["pout"], B["rcp"]], writes=[B["t2b"]])
                if cix == 0 and a == 0:
                    dump("d_atm", t2b[:, :], B["t2b"])
                for k in range(FC):
                    transpose(ptr[:, :], B["ptr"], t2b[:, k * 128:(k + 1) * 128],
                              B["t2b"], identb[:], B["identb_s"])
                    copy("v", t2b[:, k * 128:(k + 1) * 128], B["t2b"], ptr[:, :],
                         B["ptr"])
                for dk in range(FC):
                    pO_, pOb = (pO, B["pO"]) if dk % 2 == 0 else (pO1, B["pO1"])
                    for i in range(FC):
                        mm(pO_[:, :], pOb,
                           wo[:, i * E + dk * 128:i * E + dk * 128 + 128], B["wo"],
                           t2b[:, i * 128:(i + 1) * 128], B["t2b"],
                           start=(i == 0), stop=(i == FC - 1))
                    sl = slice(dk * C + a * SUB, dk * C + (a + 1) * SUB)
                    qtr = T1R[(dk * C + a * SUB) // 512]
                    stt("v", t1[:, sl], qtr, pO_[:, :], pOb, ob[:, dk:dk + 1],
                        B["ob"], t1[:, sl], qtr, ALU.add, ALU.add)
                    if a == 1:
                        ln_stats_rows(t1[:, :], T1R[(dk * C) // 512], dk, ASCR, APB)
            if cix == 0:
                dump("d_res", t1[:, :], T1R)
            ln_stats_tail(EPS6, ASCR, APB)
            for k in range(FC):
                sl = slice(k * C, (k + 1) * C)
                qtr = T1R[(k * C) // 512]
                tt("v", t1[:, sl], qtr, t1[:, sl], qtr, pbc_a[:, 0:C], B["pbc_a"],
                   ALU.subtract)
                tt("v", t1[:, sl], qtr, t1[:, sl], qtr, pbc_a[:, C:2 * C],
                   B["pbc_a"], ALU.mult)
                ts("v", t1[:, sl], qtr, t1[:, sl], qtr, gam[:, k:k + 1],
                   ALU.mult, extra=(B["gam"],))
                ts("v", t1[:, sl], qtr, t1[:, sl], qtr, bet[:, k:k + 1],
                   ALU.add, extra=(B["bet"],))
                tt("v", xs_t[:, sl], xsb, xs_t[:, sl], xsb, t1[:, sl], qtr,
                   ALU.add)

        XS = [(xs, B["xs"]), (xs2, B["xs2"])]

        def load_chunk(c):
            xt, xb = XS[c % 2]
            b = c // CPB
            trow = (c % CPB) * C
            src = dram["x"][b * E:(b + 1) * E, trow:trow + C].rearrange(
                "(k p) t -> p k t", p=128)
            dma("sd", xt[:, :].rearrange("p (k t) -> p k t", k=FC), xb, src, Buf())

        def store_chunk(c):
            xt, xb = XS[c % 2]
            b = c // CPB
            trow = (c % CPB) * C
            dst = out_h[b * E:(b + 1) * E, trow:trow + C].rearrange(
                "(k p) t -> p k t", p=128)
            dma("gd", dst, Buf(), xt[:, :].rearrange("p (k t) -> p k t", k=FC), xb)

        load_chunk(0)
        ffn_sublayer(0, 0, XS[0][0], XS[0][1], tag="f1")
        for sslot in range(NCHUNK):
            qA, qF = [], []
            SINK[0] = qA
            attn_sublayer(sslot, XS[sslot % 2][0], XS[sslot % 2][1])
            SINK[0] = qF
            if sslot >= 1:
                c = sslot - 1
                ffn_sublayer(c // CPB, (c % CPB) * C, XS[c % 2][0], XS[c % 2][1])
                store_chunk(c)
            if sslot + 1 < NCHUNK:
                c = sslot + 1
                load_chunk(c)
                ffn_sublayer(c // CPB, (c % CPB) * C, XS[c % 2][0], XS[c % 2][1])
            SINK[0] = None
            run_interleaved(qA, qF)
        c = NCHUNK - 1
        ffn_sublayer(c // CPB, (c % CPB) * C, XS[c % 2][0], XS[c % 2][1])
        store_chunk(c)

        last_gd = len(S_.ops["gd"]) - 1
        term = S_.op("g", lambda e: None)
        S_.ops["g"][term[1]] = (lambda e: None, [("gd", last_gd)])
        S_.marked["gd"].add(last_gd)

        S_.finalize()

        wms = [dict() for _ in range(5)]

        @block.vector
        def _(v):
            emit_stream(S_, sems, v, ("v",), wms[0])

        @block.scalar
        def _(s):
            emit_stream(S_, sems, s, ("s",), wms[1])

        @block.tensor
        def _(t):
            emit_stream(S_, sems, t, ("t",), wms[2])

        @block.gpsimd
        def _(g):
            emit_stream(S_, sems, g, ("g", "gd"), wms[3], force_waits=True)

        @block.sync
        def _(sy):
            emit_stream(S_, sems, sy, ("sd",), wms[4], force_waits=True)

    return nc



_NC_CACHE = [None]


def _block_ones():
    bo = np.zeros((128, 128), np.float32)
    for fc2 in range(8):
        bo[0:64, fc2 * 16 + 2 * fc2] = 1.0
        bo[64:128, fc2 * 16 + 2 * fc2 + 1] = 1.0
    return bo


def kernel(L0_attn_x, L0_attn_q_w, L0_attn_q_b, L0_attn_k_w, L0_attn_k_b,
           L0_attn_v_w, L0_attn_v_b, L0_attn_out_w, L0_attn_out_b, L0_attn_proj,
           L0_attn_gamma, L0_attn_beta, L0_ffn_w1, L0_ffn_b1, L0_ffn_w2, L0_ffn_b2):
    import jax

    x = np.asarray(L0_attn_x, np.float32)
    ncores = 8

    with jax.default_prng_impl("threefry2x32"), \
            jax.default_device(jax.devices("cpu")[0]):
        keep = np.asarray(jax.random.bernoulli(jax.random.key(42), 0.8, x.shape))
    maskv = np.where(keep, np.float32(0.625), np.float32(0.0))

    def fm(a):
        return np.ascontiguousarray(np.transpose(a, (0, 2, 1))).reshape(-1, a.shape[1])

    x_fm = fm(x)
    m_fm = fm(maskv).astype(ml_dtypes.float8_e4m3)

    def r128(v):
        v = np.asarray(v, np.float32).reshape(-1, 128)
        return np.ascontiguousarray(v.T)

    common = {
        "qw": np.ascontiguousarray(np.asarray(L0_attn_q_w, np.float32)),
        "kw": np.ascontiguousarray(np.asarray(L0_attn_k_w, np.float32)),
        "vw": np.ascontiguousarray(np.asarray(L0_attn_v_w, np.float32)),
        "ow": np.ascontiguousarray(np.asarray(L0_attn_out_w, np.float32)),
        "w1t": np.ascontiguousarray(np.asarray(L0_ffn_w1, np.float32).T),
        "w2t": np.ascontiguousarray(np.asarray(L0_ffn_w2, np.float32).T),
        "b1r": r128(L0_ffn_b1), "b2r": r128(L0_ffn_b2),
        "qbr": r128(L0_attn_q_b), "kbr": r128(L0_attn_k_b), "obr": r128(L0_attn_out_b),
        "vbrow": np.asarray(L0_attn_v_b, np.float32).reshape(1, -1),
        "gr": r128(L0_attn_gamma), "br": r128(L0_attn_beta),
        "projt": np.ascontiguousarray(np.asarray(L0_attn_proj, np.float32).T),
        "identb": np.eye(128, dtype=ml_dtypes.bfloat16),
        "trilb": np.triu(np.ones((128, 128), np.float32)).astype(ml_dtypes.bfloat16),
        "bones": _block_ones(),
    }

    in_maps = []
    for c in range(ncores):
        m = dict(common)
        m["x"] = np.ascontiguousarray(x_fm[c * NB * E:(c + 1) * NB * E, :])
        m["mask"] = np.ascontiguousarray(m_fm[c * NB * E:(c + 1) * NB * E, :])
        in_maps.append(m)

    if _NC_CACHE[0] is None:
        _NC_CACHE[0] = build_program()
    res = run_bass_kernel_spmd(_NC_CACHE[0], in_maps, core_ids=list(range(ncores)))

    outs = []
    for c in range(ncores):
        o = np.asarray(res.results[c]["out"], np.float32).reshape(NB, E, S)
        outs.append(np.transpose(o, (0, 2, 1)))
    return np.concatenate(outs, axis=0)


# version 15 (final)
# speedup vs baseline: 1.0033x; 1.0033x over previous
import numpy as np
import ml_dtypes
import contextlib

import concourse.bass as bass
import concourse.mybir as mybir
from concourse.bass_utils import run_bass_kernel_spmd

F32 = mybir.dt.float32
BF16 = mybir.dt.bfloat16
F8E4 = mybir.dt.float8e4
AF = mybir.ActivationFunctionType
ALU = mybir.AluOpType

LN2 = float(np.log(2.0))
M32 = 12582912.0
M64 = float(1.5 * 2 ** 52)

DEBUG = False

NB = 2
S = 4096
E = 1024
H = 16
D = 64
M = 256
HD = 4096
C = 256
SUB = 128
FC = E // 128
HC = HD // 128
NCHUNK = NB * S // C
CPB = S // C



class Buf:
    __slots__ = ("writer", "readers", "parent")

    def __init__(self, parent=None):
        self.writer = None
        self.readers = []
        self.parent = parent


class Sched:
    STREAMS = ("v", "s", "t", "g", "gd", "sd")
    DMA = {"gd", "sd"}

    def __init__(self):
        self.ops = {k: [] for k in self.STREAMS}
        self.order = []
        self.marked = {k: set() for k in self.STREAMS}
        self.async_ops = set()

    def _ordered(self, prod, cons):
        if prod[0] == cons[0] == "t":
            return True
        return False

    def op(self, stream, emit, reads=(), writes=()):
        reads = list(reads) + [b.parent for b in reads if b.parent is not None]
        writes = list(writes) + [b.parent for b in writes if b.parent is not None]
        idx = len(self.ops[stream])
        me = (stream, idx)
        deps = set()
        for b in reads:
            w = b.writer
            if w is not None and not self._ordered(w, me):
                deps.add(w)
        for b in writes:
            for r in b.readers:
                if not self._ordered(r, me):
                    deps.add(r)
            w = b.writer
            if w is not None and not self._ordered(w, me):
                deps.add(w)
        self.ops[stream].append((emit, sorted(deps)))
        self.order.append(me)
        for b in reads:
            b.readers.append(me)
        for b in writes:
            b.writer = me
            b.readers = []
        for d in deps:
            self.marked[d[0]].add(d[1])
        return me

    def finalize(self):
        self.cum = {}
        for st in self.STREAMS:
            n = len(self.ops[st])
            if st in self.DMA:
                self.marked[st] = set(range(n))
            cum = []
            c = 0
            for i in range(n):
                if i in self.marked[st]:
                    c += 1
                cum.append(c)
            self.cum[st] = cum
        self.batch_end = {}
        for queue in (("g", "gd"), ("sd",)):
            qops = [so for so in self.order if so[0] in queue]
            groups = []
            cur = []
            for so in qops:
                if self.ops[so[0]][so[1]][1] and cur:
                    groups.append(cur)
                    cur = []
                cur.append(so)
            if cur:
                groups.append(cur)
            for grp in groups:
                last = {}
                for st, idx in grp:
                    if st in self.DMA:
                        last[st] = idx
                for st, idx in grp:
                    if st in self.DMA:
                        self.batch_end[(st, idx)] = last[st]

    def wait_value(self, stream, idx):
        if stream in self.DMA:
            return self.cum[stream][self.batch_end[(stream, idx)]] * 16
        return self.cum[stream][idx]


def emit_stream(sched, sems, eng, streams, watermark, force_waits=False):
    mine = [so for so in sched.order if so[0] in streams]
    for st, idx in mine:
        emit, deps = sched.ops[st][idx]
        if st in sched.DMA and deps and idx > 0:
            val = sched.cum[st][idx - 1] * 16
            if watermark.get(st, 0) < val:
                eng.wait_ge(sems[st], val)
                watermark[st] = val
        for dst, didx in deps:
            val = sched.wait_value(dst, didx)
            if force_waits or watermark.get(dst, 0) < val:
                eng.wait_ge(sems[dst], val)
                if watermark.get(dst, 0) < val:
                    watermark[dst] = val
        ins = emit(eng)
        if ins is not None and idx in sched.marked[st]:
            ins.then_inc(sems[st], 16 if st in sched.DMA else 1)




def build_program():
    nc = bass.Bass("TRN2", target_bir_lowering=False)
    dram = {}

    def din(name, shape, dt=F32):
        dram[name] = nc.dram_tensor(name, shape, dt, kind="ExternalInput").ap()
        return dram[name]

    din("x", [NB * E, S])
    din("mask", [NB * E, S], F8E4)
    din("qw", [E, E]); din("kw", [E, E]); din("vw", [E, E]); din("ow", [E, E])
    din("w1t", [E, HD]); din("w2t", [HD, E])
    din("b1r", [128, HC]); din("b2r", [128, FC])
    din("qbr", [128, FC]); din("kbr", [128, FC]); din("obr", [128, FC])
    din("vbrow", [1, E])
    din("gr", [128, FC]); din("br", [128, FC])
    din("projt", [D, M])
    din("identb", [128, 128], BF16)
    din("trilb", [128, 128], BF16)
    din("bones", [128, 128])
    out_h = nc.dram_tensor("out", [NB * E, S], F32, kind="ExternalOutput").ap()
    dbg = {}
    if DEBUG:
        for nm, shape, dt in [("d_t1", [128, FC * C], F32), ("d_t1q", [128, FC * C], F8E4),
                              ("d_h8", [128, HC * C], F8E4), ("d_x1", [128, FC * C], F32),
                              ("d_t2b", [128, FC * C], BF16), ("d_qf", [128, FC * SUB], BF16),
                              ("d_kf", [128, FC * SUB], BF16), ("d_vt", [128, H * (D + 1)], BF16),
                              ("d_crow", [1, H * 128], BF16), ("d_qp", [128, 256], BF16),
                              ("d_kpf", [128, 256], BF16), ("d_kpt", [128, 256], BF16),
                              ("d_at", [128, 128], BF16), ("d_atm", [128, E], BF16),
                              ("d_x2", [128, FC * C], F32), ("d_res", [128, FC * C], F32)]:
            dbg[nm] = nc.dram_tensor(nm, shape, dt, kind="ExternalOutput").ap()

    ex = contextlib.ExitStack()
    with ex:
        B = {}

        def sb(name, shape, dt=F32):
            t = ex.enter_context(nc.sbuf_tensor(name, shape, dt))
            B[name] = Buf()
            return t

        def psum(name, shape, dt=F32):
            t = ex.enter_context(nc.psum_tensor(name, shape, dt))
            B[name] = Buf()
            return t

        wq = sb("wq", [128, FC * E], BF16)
        wk = sb("wk", [128, FC * E], BF16)
        wv = sb("wv", [128, FC * E], BF16)
        wo = sb("wo", [128, FC * E], BF16)
        w1q = sb("w1q", [128, FC * HD], F8E4)
        w2q = sb("w2q", [128, HC * E], F8E4)
        b1q = sb("b1q", [128, HC])
        b2q = sb("b2q", [128, FC])
        qb = sb("qb", [128, FC]); kb = sb("kb", [128, FC]); ob = sb("ob", [128, FC])
        gam = sb("gam", [128, FC]); bet = sb("bet", [128, FC])
        vbt = sb("vbt", [33, 512], BF16)
        projs = sb("projs", [128, M], BF16)
        identb = sb("identb_s", [128, 128], BF16)
        tril = sb("tril_s", [128, 128], BF16)
        bones = sb("bones_s", [128, 128])
        ones1 = sb("ones1", [33, 128])
        ones1b = sb("ones1b", [33, 128], BF16)
        onesc = sb("onesc", [128, 1])
        kons = sb("kons", [128, 13])
        xs = sb("xs", [128, FC * C])
        xs2 = sb("xs2", [128, FC * C])
        mkp = sb("mkp", [128, 1024], F8E4)
        MKR = [Buf(), Buf()]
        t1 = sb("t1", [128, FC * C])
        t2b = sb("t2b", [128, FC * SUB], BF16)
        t1q = sb("t1q", [128, FC * C], F8E4)
        h8 = sb("h8", [128, HC * C], F8E4)
        NSC = 6
        SC = [sb(f"sc{i}", [128, 512]) for i in range(NSC)]
        SCB = [B[f"sc{i}"] for i in range(NSC)]
        T1R = [Buf() for _ in range(4)]
        T1Q = [t1[:, i * 512:(i + 1) * 512] for i in range(4)]

        def t1bufs(c0, c1):
            return [T1R[i] for i in range(4) if i * 512 < c1 and (i + 1) * 512 > c0]
        f1 = sb("f1", [128, FC * C])
        F1R = [Buf() for _ in range(4)]
        F1Q = [f1[:, i * 512:(i + 1) * 512] for i in range(4)]
        aln_off = nc.sbuf_base
        aln = sb("aln", [128, 512])
        ALB = B["aln"]

        def sb_at(name, shape, dt, off):
            t = ex.enter_context(contextlib.nullcontext(
                nc.alloc_sbuf_tensor_at(name, shape, dt, offset=off)))
            B[name] = Buf(parent=ALB)
            return t.ap()
        qp = sb_at("qp", [128, 256], BF16, aln_off)
        kpf = sb_at("kpf", [128, 256], BF16, aln_off + 512)
        at = sb_at("at", [128, 128], BF16, aln_off + 1024)
        qf = sb("qf", [128, FC * SUB], BF16)
        kf = sb("kf", [128, FC * SUB], BF16)
        vt = sb("vt", [128, H * (D + 1)], BF16)
        kvs = sb("kvs", [128, H * 2 * (D + 1)])
        kvbh = sb("kvbh", [128, 2 * (D + 1)], BF16)
        corr = sb("corr", [16, 128], BF16)
        corr_row = sb("corr_row", [33, 8 * 128], BF16)
        corrt = sb("corrt", [128, 16])
        rcp = sb("rcp", [128, 1])

        pb0 = psum("pb0", [128, 256]); pb1 = psum("pb1", [128, 256])
        pb2 = psum("pb2", [128, 256]); pb3 = psum("pb3", [128, 256])
        pb4 = psum("pb4", [128, 512]); pb5 = psum("pb5", [128, 512])
        pb6 = psum("pb6", [128, 512]); pb7 = psum("pb7", [128, 1024], BF16)
        def view(name, parent, ap):
            B[name] = Buf(parent=B[parent])
            return ap
        pmm = [view("pmm0", "pb0", pb0[:, 0:256]), view("pmm1", "pb1", pb1[:, 0:256])]
        pm2 = [view("pm20", "pb2", pb2[:, 0:256]), view("pm21", "pb3", pb3[:, 0:256])]
        pkv = view("pkv", "pb5", pb5[:, 0:2 * (D + 1)])
        pkv1 = view("pkv1", "pb3", pb3[:, 0:2 * (D + 1)])
        pO1 = view("pO1", "pb2", pb2[:, 0:128])
        pbc = view("pbc", "pb4", pb4[:, 0:512])
        pkp = view("pkp", "pb2", pb2[:, 0:256])
        prow = view("prow", "pb4", pb4[0:1, 0:512])
        pcorr = view("pcorr", "pb5", pb5[0:16, 0:128])
        psc = view("psc", "pb5", pb5[:, 144:272])
        pout = view("pout", "pb5", pb5[:, 272:272 + D + 1])
        pO = view("pO", "pb5", pb5[:, 384:512])
        pqp = view("pqp", "pb6", pb6[:, 0:256])
        pkt = view("pkt", "pb6", pb6[:, 256:512])
        prow_a = view("prow_a", "pb6", pb6[0:1, 0:512])
        pbc_a = view("pbc_a", "pb6", pb6[:, 0:512])
        pm2row = view("pm2row", "pb3", pb3[0:1, 0:256])
        ptr = view("ptr", "pb7", pb7[:, 0:128])
        pct = view("pct", "pb7", pb7[:, 128:144])

        SETA = [(SC[i], SCB[i]) for i in range(4)]
        SETB = [(SC[4], SCB[4]), (SC[5], SCB[5]),
                (F1Q[2], F1R[2]), (F1Q[3], F1R[3])]
        VA = (F1Q[0], F1R[0])
        VB = (F1Q[1], F1R[1])

        sems = {k: ex.enter_context(nc.semaphore(f"sem_{k}")) for k in Sched.STREAMS}
        block = ex.enter_context(nc.Block())
        S_ = Sched()
        SINK = [None]

        def emit_op(stream, f, reads=(), writes=()):
            if SINK[0] is None:
                return S_.op(stream, f, reads=reads, writes=writes)
            SINK[0].append((stream, f, list(reads), list(writes)))
            return None

        import itertools

        def run_interleaved(*queues):
            queues = [q for q in queues if q]
            if not queues:
                return
            lens = [len(q) for q in queues]
            idx = [0] * len(queues)
            for _ in range(sum(lens)):
                i = min((j for j in range(len(queues)) if idx[j] < lens[j]),
                        key=lambda j: idx[j] / lens[j])
                item = queues[i][idx[i]]
                idx[i] += 1
                if SINK[0] is not None:
                    SINK[0].append(item)
                else:
                    S_.op(item[0], item[1], reads=item[2], writes=item[3])

        def _bl(x):
            if x is None:
                return []
            return list(x) if isinstance(x, (list, tuple)) else [x]

        def ts(stream, out, outb, in_, inb, s1, op0, s2=None, op1=None, extra=()):
            def f(e, out=out, in_=in_, s1=s1, s2=s2, op0=op0, op1=op1):
                if op1 is None:
                    return e.tensor_scalar(out=out, in0=in_, scalar1=s1, scalar2=None,
                                           op0=op0)
                return e.tensor_scalar(out=out, in0=in_, scalar1=s1, scalar2=s2,
                                       op0=op0, op1=op1)
            return emit_op(stream, f, reads=_bl(inb) + list(extra), writes=_bl(outb))

        def tt(stream, out, outb, a, ab, b, bb, op):
            def f(e, out=out, a=a, b=b, op=op):
                return e.tensor_tensor(out=out, in0=a, in1=b, op=op)
            return emit_op(stream, f, reads=_bl(ab) + _bl(bb), writes=_bl(outb))

        def stt(stream, out, outb, a, ab, scal, scalb, b, bb, op0, op1):
            def f(e, out=out, a=a, scal=scal, b=b, op0=op0, op1=op1):
                return e.scalar_tensor_tensor(out=out, in0=a, scalar=scal, in1=b,
                                              op0=op0, op1=op1)
            return emit_op(stream, f, reads=_bl(ab) + _bl(bb) + _bl(scalb),
                           writes=_bl(outb))

        def act(out, outb, in_, inb, func, bias=None, biasb=None, scale=None,
                scaleb=None):
            kw = {}
            rd = _bl(inb)
            if bias is not None:
                kw["bias"] = bias
                rd += _bl(biasb)
            if scale is not None:
                kw["scale"] = scale
                rd += _bl(scaleb)

            def f(e, out=out, in_=in_, func=func, kw=kw):
                return e.activation(out=out, in_=in_, func=func, **kw)
            return emit_op("s", f, reads=rd, writes=_bl(outb))

        def mm(out, outb, lhsT, lb, rhs, rb, start, stop):
            def f(e, out=out, lhsT=lhsT, rhs=rhs, start=start, stop=stop):
                return e.matmul(out, lhsT, rhs, start=start, stop=stop)
            return emit_op("t", f, reads=_bl(lb) + _bl(rb), writes=_bl(outb))

        def transpose(out, outb, in_, inb, ident, identbuf):
            def f(e, out=out, in_=in_, ident=ident):
                return e.transpose(out, in_, ident)
            return emit_op("t", f, reads=_bl(inb) + _bl(identbuf), writes=_bl(outb))

        def dma(stream, out, outb, in_, inb):
            def f(e, out=out, in_=in_):
                return e.dma_start(out=out, in_=in_)
            return emit_op(stream, f, reads=_bl(inb), writes=_bl(outb))

        def copy(stream, out, outb, in_, inb):
            if stream == "s":
                def f(e, out=out, in_=in_):
                    return e.activation(out=out, in_=in_, func=AF.Copy)
            else:
                def f(e, out=out, in_=in_):
                    return e.tensor_copy(out=out, in_=in_)
            return emit_op(stream, f, reads=_bl(inb), writes=_bl(outb))

        def memset(stream, ap, apb, val):
            def f(e, ap=ap, val=val):
                return e.memset(ap, val)
            op = S_.op(stream, f, reads=[], writes=_bl(apb))
            S_.async_ops.add(op)
            return op

        KB = B["kons"]
        GF = kons[:, 2:3]; LNT = kons[:, 3:4]; LNNEG = kons[:, 4:5]
        B170P = kons[:, 5:6]; B170N = kons[:, 6:7]
        B163P = kons[:, 7:8]; B163N = kons[:, 8:9]; B8P = kons[:, 9:10]
        EPS5 = kons[0:1, 10:11]; EPS6 = kons[0:1, 11:12]; GI = kons[:, 12:13]

        def bridge(dst, dstb, src, srcb, sA, sAb, sB, sBb):
            ts("v", sA, sAb, src, srcb, M32, ALU.add, M32, ALU.subtract)
            ts("v", sB, sBb, src, srcb, M64, ALU.add, M64, ALU.subtract)
            stt("v", dst, dstb, sA, sAb, GF, KB, sB, sBb, ALU.mult, ALU.add)

        def qsig(dst, dstb, src, srcb, W, SET, mb):
            (s1, b1_), (s2, b2_), (s3, b3_), (s4, b4_) = [
                (a[:, :W], bu) for a, bu in SET]
            bp = B163P if mb == 3 else B170P
            bn = B163N if mb == 3 else B170N
            clamp = 2.0 ** -6 if mb == 3 else 2.0 ** -14
            stt("v", s1, b1_, src, srcb, -1.0, None, src, srcb, ALU.mult, ALU.max)
            ts("v", s1, b1_, s1, b1_, clamp, ALU.max)
            act(s2, b2_, s1, b1_, AF.Ln)
            ts("v", s2, b2_, s2, b2_, 1.0 / LN2, ALU.mult, 159.5, ALU.add)
            ts("v", s3, b3_, s2, b2_, M32, ALU.add, M32, ALU.subtract)
            ts("v", s4, b4_, s2, b2_, M64, ALU.add, M64, ALU.subtract)
            stt("v", s1, b1_, s3, b3_, GF, KB, s4, b4_, ALU.mult, ALU.add)
            act(s3, b3_, s1, b1_, AF.Exp, scale=LNNEG, scaleb=KB, bias=bp, biasb=KB)
            act(s4, b4_, s1, b1_, AF.Exp, scale=LNT, scaleb=KB, bias=bn, biasb=KB)
            tt("v", s2, b2_, src, srcb, s3, b3_, ALU.mult)
            ts("v", s3, b3_, s2, b2_, M32, ALU.add, M32, ALU.subtract)
            ts("v", s1, b1_, s2, b2_, M64, ALU.add, M64, ALU.subtract)
            stt("v", s2, b2_, s3, b3_, GF, KB, s1, b1_, ALU.mult, ALU.add)
            tt("v", dst, dstb, s2, b2_, s4, b4_, ALU.mult)

        def q43_signed(dst, dstb, src, srcb, W, SET=None):
            qsig(dst, dstb, src, srcb, W, SET or SETA, 3)

        def q510_signed(dst, dstb, src, srcb, W, SET=None):
            qsig(dst, dstb, src, srcb, W, SET or SETA, 10)

        def h_chain(dst, dstb, v_, vb, W, SET):
            (s1, b1_), (s2, b2_), (s3, b3_), (s4, b4_) = [
                (a[:, :W], bu) for a, bu in SET]
            ts("v", s1, b1_, v_, vb, 2.0 ** -14, ALU.max)
            act(s2, b2_, s1, b1_, AF.Ln)
            ts("v", s2, b2_, s2, b2_, 1.0 / LN2, ALU.mult, 159.5, ALU.add)
            ts("v", s3, b3_, s2, b2_, M32, ALU.add, M32, ALU.subtract)
            ts("v", s4, b4_, s2, b2_, M64, ALU.add, M64, ALU.subtract)
            stt("v", s1, b1_, s3, b3_, GF, KB, s4, b4_, ALU.mult, ALU.add)
            act(s3, b3_, s1, b1_, AF.Exp, scale=LNNEG, scaleb=KB, bias=B170P, biasb=KB)
            tt("v", v_, vb, v_, vb, s3, b3_, ALU.mult)
            ts("v", s2, b2_, v_, vb, M32, ALU.add, M32, ALU.subtract)
            ts("v", s4, b4_, v_, vb, M64, ALU.add, M64, ALU.subtract)
            stt("v", s3, b3_, s2, b2_, GF, KB, s4, b4_, ALU.mult, ALU.add)
            ts("v", s2, b2_, s1, b1_, -154.0, ALU.add, 0.0, ALU.min)
            ts("v", s4, b4_, s1, b1_, 154.0, ALU.max)
            act(s2, b2_, s2, b2_, AF.Exp, scale=LNT, scaleb=KB, bias=B8P, biasb=KB)
            ts("v", s1, b1_, s2, b2_, M32, ALU.add, M32, ALU.subtract)
            ts("v", v_, vb, s2, b2_, M64, ALU.add, M64, ALU.subtract)
            stt("v", s2, b2_, s1, b1_, GF, KB, v_, vb, ALU.mult, ALU.add)
            act(s4, b4_, s4, b4_, AF.Exp, scale=LNT, scaleb=KB, bias=B163N, biasb=KB)
            tt("v", s1, b1_, s3, b3_, s2, b2_, ALU.mult)
            ts("v", s1, b1_, s1, b1_, 2.0 ** -15, ALU.mult)
            ts("v", s3, b3_, s1, b1_, M32, ALU.add, M32, ALU.subtract)
            ts("v", s2, b2_, s1, b1_, M64, ALU.add, M64, ALU.subtract)
            stt("v", s1, b1_, s3, b3_, GF, KB, s2, b2_, ALU.mult, ALU.add)
            tt("v", dst, dstb, s1, b1_, s4, b4_, ALU.mult)

        def ln_stats(x_t, xb, eps_ap, SCR=None, PB=None):
            if SCR is None:
                SCR = ((SC[0][:, 0:C], SCB[0]), (SC[1][:, 0:C], SCB[1]),
                       (SC[2][0:1, 0:512], SCB[2]))
            (sq0, sq0b), (sq1, sq1b), (r2t, r2b) = SCR[0], SCR[1], (SCR[2][0], SCR[2][1])
            mu_, mub, m2_, m2b, pbc_, pbcb = PB or (
                prow[0:1, 0:C], B["prow"], prow[0:1, C:2 * C], B["prow"],
                pbc, B["pbc"])
            def xq(k):
                return xb[(k * C) // 512] if isinstance(xb, list) else xb
            for k in range(FC):
                mm(mu_, mub, onesc[:, 0:1], B["onesc"],
                   x_t[:, k * C:(k + 1) * C], xq(k), start=(k == 0), stop=(k == FC - 1))
            for k in range(FC):
                sq, sqb = (sq0, sq0b) if k % 2 == 0 else (sq1, sq1b)
                act(sq, sqb, x_t[:, k * C:(k + 1) * C], xq(k), AF.Square)
                mm(m2_, m2b, onesc[:, 0:1], B["onesc"],
                   sq, sqb, start=(k == 0), stop=(k == FC - 1))
            ln_stats_tail(eps_ap, SCR, (mu_, mub, m2_, m2b, pbc_, pbcb))

        def ln_stats_rows(x_t, xb, k, SCR, PB):
            (sq0, sq0b), (sq1, sq1b), _ = SCR
            mu_, mub, m2_, m2b, pbc_, pbcb = PB
            mm(mu_, mub, onesc[:, 0:1], B["onesc"],
               x_t[:, k * C:(k + 1) * C], xb, start=(k == 0), stop=(k == FC - 1))
            sq, sqb = (sq0, sq0b) if k % 2 == 0 else (sq1, sq1b)
            act(sq, sqb, x_t[:, k * C:(k + 1) * C], xb, AF.Square)
            mm(m2_, m2b, onesc[:, 0:1], B["onesc"],
               sq, sqb, start=(k == 0), stop=(k == FC - 1))

        def ln_stats_tail(eps_ap, SCR, PB):
            (sq0, sq0b), (sq1, sq1b), (r2t, r2b) = SCR[0], SCR[1], (SCR[2][0], SCR[2][1])
            mu_, mub, m2_, m2b, pbc_, pbcb = PB
            def a_mu(e):
                return e.activation(out=r2t[:, 0:C], in_=mu_,
                                    func=AF.Copy, scale=1.0 / 1024)
            emit_op("s", a_mu, reads=[mub], writes=[r2b])
            def a_m2(e):
                return e.activation(out=r2t[:, C:2 * C], in_=m2_,
                                    func=AF.Copy, scale=1.0 / 1024)
            emit_op("s", a_m2, reads=[m2b], writes=[r2b])
            mm(pbc_[:, 0:C], pbcb, ones1[0:1, :], B["ones1"], r2t[:, 0:C], r2b,
               start=True, stop=True)
            tt("v", r2t[:, 0:C], r2b, r2t[:, 0:C], r2b,
               r2t[:, 0:C], r2b, ALU.mult)
            tt("v", r2t[:, 0:C], r2b, r2t[:, C:2 * C], r2b,
               r2t[:, 0:C], r2b, ALU.subtract)
            act(r2t[:, 0:C], r2b, r2t[:, 0:C], r2b,
                AF.Ln, bias=eps_ap, biasb=KB)
            def a_rstd(e):
                return e.activation(out=r2t[:, 0:C], in_=r2t[:, 0:C],
                                    func=AF.Exp, scale=-0.5)
            emit_op("s", a_rstd, reads=[r2b], writes=[r2b])
            mm(pbc_[:, C:2 * C], pbcb, ones1[0:1, :], B["ones1"], r2t[:, 0:C],
               r2b, start=True, stop=True)

        def ln_normalize(dst_t, dstb, x_t, xb, PB=None):
            pbc_, pbcb = (PB[4], PB[5]) if PB else (pbc, B["pbc"])
            def bq(bb, k):
                return bb[(k * C) // 512] if isinstance(bb, list) else bb
            for k in range(FC):
                sl = slice(k * C, (k + 1) * C)
                tt("v", dst_t[:, sl], bq(dstb, k), x_t[:, sl], bq(xb, k),
                   pbc_[:, 0:C], pbcb, ALU.subtract)
                tt("v", dst_t[:, sl], bq(dstb, k), dst_t[:, sl], bq(dstb, k),
                   pbc_[:, C:2 * C], pbcb, ALU.mult)

        memset("v", kons[:, 0:1], KB, 2.0)
        memset("v", kons[:, 1:2], KB, 1.0)
        ts("v", kons[:, 2:3], KB, kons[:, 1:2], KB, 2.0 ** -40, ALU.add, 1.0, ALU.is_equal)
        ts("v", kons[:, 12:13], KB, kons[:, 2:3], KB, -1.0, ALU.mult, 1.0, ALU.add)
        act(kons[:, 3:4], KB, kons[:, 0:1], KB, AF.Ln)
        for col, mul in [(4, -1.0), (5, 170.0), (6, -170.0), (7, 163.0), (8, -163.0),
                         (9, 8.0)]:
            ts("v", kons[:, col:col + 1], KB, kons[:, 3:4], KB, mul, ALU.mult)
        memset("v", kons[:, 10:11], KB, 1e-5)
        memset("v", kons[:, 11:12], KB, 1e-6)
        memset("v", ones1[0:1, :], B["ones1"], 1.0)
        memset("v", ones1[32:33, :], B["ones1"], 1.0)
        memset("v", ones1b[0:1, :], B["ones1b"], 1.0)
        memset("v", ones1b[32:33, :], B["ones1b"], 1.0)
        memset("v", onesc[:], B["onesc"], 1.0)

        for name, tile, tb in [("identb", identb, B["identb_s"]),
                               ("trilb", tril, B["tril_s"]),
                               ("bones", bones, B["bones_s"]),
                               ("qbr", qb, B["qb"]), ("kbr", kb, B["kb"]),
                               ("obr", ob, B["ob"]), ("gr", gam, B["gam"]),
                               ("br", bet, B["bet"])]:
            dma("sd", tile[:], tb, dram[name], Buf())

        dma("sd", SC[3][0:1, 0:512], SCB[3], dram["vbrow"][0:1, 0:512], Buf())
        dma("sd", SC[4][0:1, 0:512], SCB[4], dram["vbrow"][0:1, 512:1024], Buf())
        copy("s", vbt[0:1, :], B["vbt"], SC[3][0:1, 0:512], SCB[3])
        copy("s", vbt[32:33, :], B["vbt"], SC[4][0:1, 0:512], SCB[4])

        dma("sd", SC[5][0:64, 0:M], SCB[5], dram["projt"], Buf())
        dma("sd", SC[5][64:128, 0:M], SCB[5], dram["projt"], Buf())
        ts("v", projs[:, :], B["projs"], SC[5][:, 0:M], SCB[5], 0.125, ALU.mult)

        dma("sd", SC[5][:, 256:256 + HC], SCB[5], dram["b1r"], Buf())
        q43_signed(b1q[:, :], B["b1q"], SC[5][:, 256:256 + HC], SCB[5], HC)
        dma("sd", SC[5][:, 256:256 + FC], SCB[5], dram["b2r"], Buf())
        q43_signed(b2q[:, :], B["b2q"], SC[5][:, 256:256 + FC], SCB[5], FC)

        for wname, wtile, wb in [("qw", wq, B["wq"]), ("kw", wk, B["wk"]),
                                 ("vw", wv, B["wv"]), ("ow", wo, B["wo"])]:
            for i in range(0, FC, 2):
                src = dram[wname][i * 128:(i + 2) * 128, :].rearrange(
                    "(k p) n -> p k n", p=128)
                dma("sd", xs[:, 0:2048].rearrange("p (k n) -> p k n", k=2), B["xs"],
                    src, Buf())
                copy("v", wtile[:, i * E:(i + 2) * E], wb, xs[:, 0:2048], B["xs"])

        def wquant4(dst_tile, dstb, col0):
            outer = SINK[0]
            qa, qb2 = [], []
            for q in range(4):
                SINK[0] = qa if q % 2 == 0 else qb2
                q43_signed(dst_tile[:, col0 + q * 512:col0 + (q + 1) * 512], dstb,
                           xs[:, q * 512:(q + 1) * 512], B["xs"], 512,
                           SET=(SETA if q % 2 == 0 else SETB))
            SINK[0] = outer
            run_interleaved(qa, qb2)

        for i in range(FC):
            for half in range(2):
                dma("sd", xs[:, 0:2048], B["xs"],
                    dram["w1t"][i * 128:(i + 1) * 128, half * 2048:(half + 1) * 2048],
                    Buf())
                wquant4(w1q, B["w1q"], i * HD + half * 2048)
        for j in range(0, HC, 2):
            src = dram["w2t"][j * 128:(j + 2) * 128, :].rearrange(
                "(k p) n -> p k n", p=128)
            dma("sd", xs[:, 0:2048].rearrange("p (k n) -> p k n", k=2), B["xs"],
                src, Buf())
            wquant4(w2q, B["w2q"], j * E)

        for h in range(H):
            memset("v", vt[:, h * (D + 1) + D:h * (D + 1) + D + 1], B["vt"], 1.0)

        def dump(nm, tile, tileb):
            if DEBUG and nm in dbg and nm not in dump.done:
                dump.done.add(nm)
                dma("gd", dbg[nm], Buf(), tile, tileb)
        dump.done = set()

        def ffn_sublayer(b, trow, xs_t, xsb, tag=""):
            ln_stats(xs_t[:, :], xsb, EPS5)
            ln_normalize(f1[:, :], F1R, xs_t[:, :], xsb)
            if tag == "f1":
                dump("d_t1", f1[:, :], F1R)
            for q in (2, 3):
                q43_signed(t1q[:, q * 512:(q + 1) * 512], B["t1q"],
                           F1Q[q], F1R[q], 512, SET=SETA)
            outer = SINK[0]
            qa, qb2 = [], []
            SINK[0] = qa
            q43_signed(t1q[:, 0:512], B["t1q"], F1Q[0], F1R[0], 512, SET=SETA)
            SINK[0] = qb2
            q43_signed(t1q[:, 512:1024], B["t1q"], F1Q[1], F1R[1], 512, SET=SETB)
            SINK[0] = outer
            run_interleaved(qa, qb2)
            if tag == "f1":
                dump("d_t1q", t1q[:, :], B["t1q"])
            for j in range(0, HC, 4):
                for pair, (vt_, vb_) in ((0, VA), (2, VB)):
                    for jj in (j + pair, j + pair + 1):
                        p = pmm[jj % 2]
                        pb = B[f"pmm{jj % 2}"]
                        for i in range(FC):
                            mm(p[:, 0:C], pb,
                               w1q[:, i * HD + jj * 128:i * HD + jj * 128 + 128],
                               B["w1q"], t1q[:, i * C:(i + 1) * C], B["t1q"],
                               start=(i == 0), stop=(i == FC - 1))
                        half = jj - j - pair
                        def ev(e, p=p, jj=jj, half=half, vt_=vt_):
                            return e.activation(out=vt_[:, half * C:(half + 1) * C],
                                                in_=p[:, 0:C], func=AF.Relu,
                                                bias=b1q[:, jj:jj + 1])
                        emit_op("s", ev, reads=[pb, B["b1q"]], writes=[vb_])
                outer = SINK[0]
                qa, qb2 = [], []
                SINK[0] = qa
                h_chain(h8[:, j * C:(j + 2) * C], B["h8"], VA[0], VA[1], 2 * C, SETA)
                SINK[0] = qb2
                h_chain(h8[:, (j + 2) * C:(j + 4) * C], B["h8"], VB[0], VB[1], 2 * C,
                        SETB)
                SINK[0] = outer
                run_interleaved(qa, qb2)
            if tag == "f1":
                dump("d_h8", h8[:, :], B["h8"])
            for d in (0, 4):
                for pair, (vt_, vb_), mh in ((0, VA, 0), (2, VB, 1)):
                    for dd in (d + pair, d + pair + 1):
                        p = pmm[dd % 2]
                        pb = B[f"pmm{dd % 2}"]
                        for jj in range(HC):
                            mm(p[:, 0:C], pb,
                               w2q[:, jj * E + dd * 128:jj * E + dd * 128 + 128],
                               B["w2q"], h8[:, jj * C:(jj + 1) * C], B["h8"],
                               start=(jj == 0), stop=(jj == HC - 1))
                        half = dd - d - pair
                        def ev2(e, p=p, dd=dd, half=half, vt_=vt_):
                            return e.activation(out=vt_[:, half * C:(half + 1) * C],
                                                in_=p[:, 0:C], func=AF.Identity,
                                                bias=b2q[:, dd:dd + 1])
                        emit_op("s", ev2, reads=[pb, B["b2q"]], writes=[vb_])
                    msrc = dram["mask"][b * E + (d + pair) * 128:
                                        b * E + (d + pair + 2) * 128,
                                        trow:trow + C].rearrange(
                        "(k p) t -> p k t", p=128)
                    dma("sd", mkp[:, mh * 512:(mh + 1) * 512].rearrange(
                        "p (k t) -> p k t", k=2), MKR[mh], msrc, Buf())
                outer = SINK[0]
                qa, qb2 = [], []
                for pair, (vt_, vb_), mh, st_, q_ in ((0, VA, 0, SETA, 0),
                                                      (2, VB, 1, SETB, 1)):
                    SINK[0] = qa if q_ == 0 else qb2
                    q510_signed(vt_[:, 0:2 * C], vb_, vt_[:, 0:2 * C], vb_, 2 * C,
                                SET=st_)
                    tt("g", vt_[:, 0:2 * C], vb_, vt_[:, 0:2 * C], vb_,
                       mkp[:, mh * 512:(mh + 1) * 512], MKR[mh], ALU.mult)
                    sl = slice((d + pair) * C, (d + pair + 2) * C)
                    tt("g", xs_t[:, sl], xsb, xs_t[:, sl], xsb,
                       vt_[:, 0:2 * C], vb_, ALU.add)
                SINK[0] = outer
                run_interleaved(qa, qb2)

        ASCR = ((aln[:, 0:C], ALB), (aln[:, C:2 * C], ALB),
                (aln[0:1, 0:512], ALB))

        def attn_sublayer(cix, xs_t, xsb):
            batch_start = (cix % CPB) == 0
            if cix == 0:
                dump("d_x1", xs_t[:, :], xsb)
            APB = (prow_a[0:1, 0:C], B["prow_a"], pm2row[0:1, 0:C], B["pm2row"],
                   pbc_a, B["pbc_a"])
            ln_stats(xs_t[:, :], xsb, EPS5, SCR=ASCR, PB=APB)
            ln_normalize(t1[:, :], T1R, xs_t[:, :], xsb, PB=APB)
            for a in range(C // SUB):
                first = batch_start and a == 0
                copy("s", t2b[:, :].rearrange("p (k t) -> p k t", t=SUB), B["t2b"],
                     t1[:, :].rearrange("p (k t) -> p k t", t=C)[:, :, a * SUB:(a + 1) * SUB],
                     T1R)
                if cix == 0 and a == 0:
                    dump("d_t2b", t2b[:, :], B["t2b"])
                for wtile, wb, btile, bb, dstt, dstb in (
                        (wq, B["wq"], qb, B["qb"], qf, B["qf"]),
                        (wk, B["wk"], kb, B["kb"], kf, B["kf"])):
                    for dk in range(FC):
                        p = pm2[dk % 2]
                        pb = B[f"pm2{dk % 2}"]
                        for i in range(FC):
                            mm(p[:, 0:SUB], pb,
                               wtile[:, i * E + dk * 128:i * E + dk * 128 + 128], wb,
                               t2b[:, i * SUB:(i + 1) * SUB], B["t2b"],
                               start=(i == 0), stop=(i == FC - 1))
                        def evq(e, p=p, dk=dk, dstt=dstt, btile=btile):
                            return e.activation(out=dstt[:, dk * SUB:(dk + 1) * SUB],
                                                in_=p[:, 0:SUB], func=AF.Identity,
                                                bias=btile[:, dk:dk + 1])
                        emit_op("s", evq, reads=[pb, bb], writes=[dstb])
                for half in range(2):
                    vbase = 32 * half
                    for q2 in range(2):
                        p = pm2[q2]
                        pbq = B[f"pm2{q2}"]
                        off = half * 512 + q2 * 256
                        for i in range(FC):
                            mm(p[:, 0:256], pbq,
                               t2b[:, i * SUB:(i + 1) * SUB], B["t2b"],
                               wv[:, i * E + off:i * E + off + 256], B["wv"],
                               start=(i == 0), stop=False)
                        mm(p[:, 0:256], pbq,
                           ones1b[vbase:vbase + 1, :], B["ones1b"],
                           vbt[vbase:vbase + 1, q2 * 256:(q2 + 1) * 256], B["vbt"],
                           start=False, stop=True)
                        hb = half * 8 + q2 * 4
                        dstv = vt[:, hb * (D + 1):(hb + 4) * (D + 1)].rearrange(
                            "p (h d) -> p h d", d=D + 1)[:, :, 0:D]
                        def evv(e, p=p, dstv=dstv):
                            return e.activation(
                                out=dstv,
                                in_=p[:, 0:256].rearrange("p (h d) -> p h d", d=D),
                                func=AF.Copy)
                        emit_op("s", evv, reads=[pbq], writes=[B["vt"]])
                if cix == 0 and a == 0:
                    dump("d_qf", qf[:, :], B["qf"])
                    dump("d_kf", kf[:, :], B["kf"])
                    dump("d_vt", vt[:, :], B["vt"])
                for fc2 in range(FC):
                    sqk = aln[:, (fc2 % 2) * 128:(fc2 % 2) * 128 + 128]
                    act(sqk, ALB, kf[:, fc2 * 128:(fc2 + 1) * 128], B["kf"],
                        AF.Square)
                    mm(pcorr[:, :], B["pcorr"], bones[:, fc2 * 16:(fc2 + 1) * 16],
                       B["bones_s"], sqk, ALB,
                       start=(fc2 == 0), stop=(fc2 == FC - 1))
                def evc(e):
                    return e.activation(out=corr[:, :], in_=pcorr[:, :], func=AF.Copy,
                                        scale=-1.0 / 128)
                emit_op("s", evc, reads=[B["pcorr"]], writes=[B["corr"]])
                dma("gd", corr_row[0:1, :], B["corr_row"], corr[0:8, :], B["corr"])
                dma("gd", corr_row[32:33, :], B["corr_row"], corr[8:16, :], B["corr"])
                transpose(pct[:, :], B["pct"], corr[:, :], B["corr"],
                          identb[0:16, 0:16], B["identb_s"])
                copy("s", corrt[:, :], B["corrt"], pct[:, :], B["pct"])
                for h in range(H):
                    fc2 = h // 2
                    base = 64 * (h % 2)
                    Qh = qf[base:base + 64, fc2 * SUB:(fc2 + 1) * SUB]
                    Kh = kf[base:base + 64, fc2 * SUB:(fc2 + 1) * SUB]
                    ksl = slice(h * 2 * (D + 1), (h + 1) * 2 * (D + 1))
                    if not first:
                        copy("s", kvbh[:, :], B["kvbh"], kvs[:, ksl], B["kvs"])
                    for mc in range(2):
                        mm(pqp[:, mc * 128:(mc + 1) * 128], B["pqp"],
                           projs[base:base + 64, mc * 128:(mc + 1) * 128], B["projs"],
                           Qh, B["qf"], start=True, stop=True)
                    act(qp[:, 0:256], B["qp"], pqp[:, 0:256], B["pqp"], AF.Exp)
                    for mc in range(2):
                        mm(pkp[:, mc * 128:(mc + 1) * 128], B["pkp"],
                           projs[base:base + 64, mc * 128:(mc + 1) * 128], B["projs"],
                           Kh, B["kf"], start=True, stop=False)
                        cbase = 32 * (h // 8)
                        mm(pkp[:, mc * 128:(mc + 1) * 128], B["pkp"],
                           ones1b[cbase:cbase + 1, :], B["ones1b"],
                           corr_row[cbase:cbase + 1, (h % 8) * 128:(h % 8) * 128 + 128],
                           B["corr_row"], start=False, stop=True)
                    act(kpf[:, 0:256], B["kpf"], pkp[:, 0:256], B["pkp"], AF.Exp)
                    mm(pkt[:, 0:256], B["pkt"], Kh, B["kf"],
                       projs[base:base + 64, 0:256], B["projs"], start=True, stop=True)
                    for mc in range(2):
                        mm(psc[:, :], B["psc"], kpf[:, mc * 128:(mc + 1) * 128], B["kpf"],
                           qp[:, mc * 128:(mc + 1) * 128], B["qp"],
                           start=(mc == 0), stop=(mc == 1))
                    tt("v", at[:, :], B["at"], psc[:, :], B["psc"], tril[:],
                       B["tril_s"], ALU.mult)
                    act(kpf[:, 0:256], B["kpf"], pkt[:, 0:256], B["pkt"], AF.Exp,
                        bias=corrt[:, h:h + 1], biasb=B["corrt"])
                    if cix == 0 and a == 0 and h == 0:
                        dump("d_qp", qp[:, :], B["qp"])
                        dump("d_at", at[:, :], B["at"])
                    Vh = vt[:, h * (D + 1):(h + 1) * (D + 1)]
                    mm(pout[:, :], B["pout"], at[:, :], B["at"], Vh, B["vt"],
                       start=True, stop=first)
                    if not first:
                        for mc in range(2):
                            mm(pout[:, :], B["pout"], qp[:, mc * 128:(mc + 1) * 128],
                               B["qp"],
                               kvbh[:, mc * (D + 1):(mc + 1) * (D + 1)],
                               B["kvbh"], start=False, stop=(mc == 1))
                    pkv_, pkvb = (pkv, B["pkv"]) if h % 2 == 0 else (pkv1, B["pkv1"])
                    for mc in range(2):
                        mm(pkv_[:, mc * (D + 1):(mc + 1) * (D + 1)], pkvb,
                           kpf[:, mc * 128:(mc + 1) * 128], B["kpf"], Vh, B["vt"],
                           start=True, stop=True)
                    if first:
                        copy("v", kvs[:, ksl], B["kvs"], pkv_[:, :], pkvb)
                    else:
                        tt("v", kvs[:, ksl], B["kvs"], kvs[:, ksl], B["kvs"],
                           pkv_[:, :], pkvb, ALU.add)
                    def evr(e):
                        return e.reciprocal(out=rcp[:, :], in_=pout[:, D:D + 1])
                    emit_op("v", evr, reads=[B["pout"]], writes=[B["rcp"]])
                    def evo(e, h=h):
                        return e.activation(out=t2b[:, h * D:(h + 1) * D],
                                            in_=pout[:, 0:D], func=AF.Copy,
                                            scale=rcp[:, 0:1])
                    emit_op("s", evo, reads=[B["pout"], B["rcp"]], writes=[B["t2b"]])
                if cix == 0 and a == 0:
                    dump("d_atm", t2b[:, :], B["t2b"])
                for k in range(FC):
                    transpose(ptr[:, :], B["ptr"], t2b[:, k * 128:(k + 1) * 128],
                              B["t2b"], identb[:], B["identb_s"])
                    copy("v", t2b[:, k * 128:(k + 1) * 128], B["t2b"], ptr[:, :],
                         B["ptr"])
                for dk in range(FC):
                    pO_, pOb = (pO, B["pO"]) if dk % 2 == 0 else (pO1, B["pO1"])
                    for i in range(FC):
                        mm(pO_[:, :], pOb,
                           wo[:, i * E + dk * 128:i * E + dk * 128 + 128], B["wo"],
                           t2b[:, i * 128:(i + 1) * 128], B["t2b"],
                           start=(i == 0), stop=(i == FC - 1))
                    sl = slice(dk * C + a * SUB, dk * C + (a + 1) * SUB)
                    qtr = T1R[(dk * C + a * SUB) // 512]
                    stt("v", t1[:, sl], qtr, pO_[:, :], pOb, ob[:, dk:dk + 1],
                        B["ob"], t1[:, sl], qtr, ALU.add, ALU.add)
                    if a == 1:
                        ln_stats_rows(t1[:, :], T1R[(dk * C) // 512], dk, ASCR, APB)
            if cix == 0:
                dump("d_res", t1[:, :], T1R)
            ln_stats_tail(EPS6, ASCR, APB)
            for k in range(FC):
                sl = slice(k * C, (k + 1) * C)
                qtr = T1R[(k * C) // 512]
                tt("v", t1[:, sl], qtr, t1[:, sl], qtr, pbc_a[:, 0:C], B["pbc_a"],
                   ALU.subtract)
                tt("v", t1[:, sl], qtr, t1[:, sl], qtr, pbc_a[:, C:2 * C],
                   B["pbc_a"], ALU.mult)
                ts("v", t1[:, sl], qtr, t1[:, sl], qtr, gam[:, k:k + 1],
                   ALU.mult, extra=(B["gam"],))
                ts("v", t1[:, sl], qtr, t1[:, sl], qtr, bet[:, k:k + 1],
                   ALU.add, extra=(B["bet"],))
                tt("v", xs_t[:, sl], xsb, xs_t[:, sl], xsb, t1[:, sl], qtr,
                   ALU.add)

        XS = [(xs, B["xs"]), (xs2, B["xs2"])]

        def load_chunk(c):
            xt, xb = XS[c % 2]
            b = c // CPB
            trow = (c % CPB) * C
            src = dram["x"][b * E:(b + 1) * E, trow:trow + C].rearrange(
                "(k p) t -> p k t", p=128)
            dma("sd", xt[:, :].rearrange("p (k t) -> p k t", k=FC), xb, src, Buf())

        def store_chunk(c):
            xt, xb = XS[c % 2]
            b = c // CPB
            trow = (c % CPB) * C
            dst = out_h[b * E:(b + 1) * E, trow:trow + C].rearrange(
                "(k p) t -> p k t", p=128)
            dma("gd", dst, Buf(), xt[:, :].rearrange("p (k t) -> p k t", k=FC), xb)

        load_chunk(0)
        ffn_sublayer(0, 0, XS[0][0], XS[0][1], tag="f1")
        for sslot in range(NCHUNK):
            qA, qF = [], []
            SINK[0] = qA
            attn_sublayer(sslot, XS[sslot % 2][0], XS[sslot % 2][1])
            SINK[0] = qF
            if sslot >= 1:
                c = sslot - 1
                ffn_sublayer(c // CPB, (c % CPB) * C, XS[c % 2][0], XS[c % 2][1])
                store_chunk(c)
            if sslot + 1 < NCHUNK:
                c = sslot + 1
                load_chunk(c)
                ffn_sublayer(c // CPB, (c % CPB) * C, XS[c % 2][0], XS[c % 2][1])
            SINK[0] = None
            run_interleaved(qA, qF)
        c = NCHUNK - 1
        ffn_sublayer(c // CPB, (c % CPB) * C, XS[c % 2][0], XS[c % 2][1])
        store_chunk(c)

        last_gd = len(S_.ops["gd"]) - 1
        term = S_.op("g", lambda e: None)
        S_.ops["g"][term[1]] = (lambda e: None, [("gd", last_gd)])
        S_.marked["gd"].add(last_gd)

        S_.finalize()

        wms = [dict() for _ in range(5)]

        @block.vector
        def _(v):
            emit_stream(S_, sems, v, ("v",), wms[0])

        @block.scalar
        def _(s):
            emit_stream(S_, sems, s, ("s",), wms[1])

        @block.tensor
        def _(t):
            emit_stream(S_, sems, t, ("t",), wms[2])

        @block.gpsimd
        def _(g):
            emit_stream(S_, sems, g, ("g", "gd"), wms[3], force_waits=True)

        @block.sync
        def _(sy):
            emit_stream(S_, sems, sy, ("sd",), wms[4], force_waits=True)

    return nc



_NC_CACHE = [None]


def _block_ones():
    bo = np.zeros((128, 128), np.float32)
    for fc2 in range(8):
        bo[0:64, fc2 * 16 + 2 * fc2] = 1.0
        bo[64:128, fc2 * 16 + 2 * fc2 + 1] = 1.0
    return bo


def kernel(L0_attn_x, L0_attn_q_w, L0_attn_q_b, L0_attn_k_w, L0_attn_k_b,
           L0_attn_v_w, L0_attn_v_b, L0_attn_out_w, L0_attn_out_b, L0_attn_proj,
           L0_attn_gamma, L0_attn_beta, L0_ffn_w1, L0_ffn_b1, L0_ffn_w2, L0_ffn_b2):
    import jax

    x = np.asarray(L0_attn_x, np.float32)
    ncores = 8

    import os as _os
    _x64 = _os.environ.get("BASS_SIM_FP64") == "1"
    with jax.default_prng_impl("threefry2x32"), \
            jax.default_device(jax.devices("cpu")[0]):
        if _x64:
            from jax.experimental import enable_x64
            with enable_x64():
                keep = np.asarray(
                    jax.random.bernoulli(jax.random.key(42), 0.8, x.shape))
        else:
            keep = np.asarray(jax.random.bernoulli(jax.random.key(42), 0.8, x.shape))
    maskv = np.where(keep, np.float32(0.625), np.float32(0.0))

    def fm(a):
        return np.ascontiguousarray(np.transpose(a, (0, 2, 1))).reshape(-1, a.shape[1])

    x_fm = fm(x)
    m_fm = fm(maskv).astype(ml_dtypes.float8_e4m3)

    def r128(v):
        v = np.asarray(v, np.float32).reshape(-1, 128)
        return np.ascontiguousarray(v.T)

    common = {
        "qw": np.ascontiguousarray(np.asarray(L0_attn_q_w, np.float32)),
        "kw": np.ascontiguousarray(np.asarray(L0_attn_k_w, np.float32)),
        "vw": np.ascontiguousarray(np.asarray(L0_attn_v_w, np.float32)),
        "ow": np.ascontiguousarray(np.asarray(L0_attn_out_w, np.float32)),
        "w1t": np.ascontiguousarray(np.asarray(L0_ffn_w1, np.float32).T),
        "w2t": np.ascontiguousarray(np.asarray(L0_ffn_w2, np.float32).T),
        "b1r": r128(L0_ffn_b1), "b2r": r128(L0_ffn_b2),
        "qbr": r128(L0_attn_q_b), "kbr": r128(L0_attn_k_b), "obr": r128(L0_attn_out_b),
        "vbrow": np.asarray(L0_attn_v_b, np.float32).reshape(1, -1),
        "gr": r128(L0_attn_gamma), "br": r128(L0_attn_beta),
        "projt": np.ascontiguousarray(np.asarray(L0_attn_proj, np.float32).T),
        "identb": np.eye(128, dtype=ml_dtypes.bfloat16),
        "trilb": np.triu(np.ones((128, 128), np.float32)).astype(ml_dtypes.bfloat16),
        "bones": _block_ones(),
    }

    in_maps = []
    for c in range(ncores):
        m = dict(common)
        m["x"] = np.ascontiguousarray(x_fm[c * NB * E:(c + 1) * NB * E, :])
        m["mask"] = np.ascontiguousarray(m_fm[c * NB * E:(c + 1) * NB * E, :])
        in_maps.append(m)

    if _NC_CACHE[0] is None:
        _NC_CACHE[0] = build_program()
    res = run_bass_kernel_spmd(_NC_CACHE[0], in_maps, core_ids=list(range(ncores)))

    outs = []
    for c in range(ncores):
        o = np.asarray(res.results[c]["out"], np.float32).reshape(NB, E, S)
        outs.append(np.transpose(o, (0, 2, 1)))
    return np.concatenate(outs, axis=0)
